# version 10
# baseline (speedup 1.0000x reference)
import contextlib
import numpy as np

import concourse.bass as bass
import concourse.mybir as mybir
from concourse.bass_utils import run_bass_kernel_spmd

F32 = mybir.dt.float32
BF16 = mybir.dt.bfloat16
AX = mybir.AxisListType
OP = mybir.AluOpType
ACT = mybir.ActivationFunctionType

T = 2048
D = 1024
O = 3072
NT = T // 128
ND = D // 128
NO = O // 128
NC = 8
RG = [list(range(NC))]

CR = float(3 << 22)
INV_QMAX = float(np.float32(1.0) / np.float32(32767.0))
WQMAX_INV = float(np.float32(1.0) / np.float32(7.0))
MASK_NEG = -1.0e5


def build(debug=False, probe=False):
    nc = bass.Bass("TRN2", target_bir_lowering=False)

    x_d = nc.dram_tensor("x", [T, D], F32, kind="ExternalInput").ap()
    xT_d = nc.dram_tensor("xT", [D, T], F32, kind="ExternalInput").ap()
    wq_d = nc.dram_tensor("wq", [D, D], F32, kind="ExternalInput").ap()
    wk_d = nc.dram_tensor("wk", [D, D], F32, kind="ExternalInput").ap()
    wv_d = nc.dram_tensor("wv", [D, D], F32, kind="ExternalInput").ap()
    biases_d = nc.dram_tensor("biases", [65, D], F32, kind="ExternalInput").ap()
    wup_d = nc.dram_tensor("wup_s", [O, 128], F32, kind="ExternalInput").ap()
    wgate_d = nc.dram_tensor("wgate_s", [O, 128], F32, kind="ExternalInput").ap()
    wdown_d = nc.dram_tensor("wdown_s", [D, O // NC], F32, kind="ExternalInput").ap()
    mask_d = nc.dram_tensor("mask", [128, 128], BF16, kind="ExternalInput").ap()
    ident_d = nc.dram_tensor("ident", [128, 128], BF16, kind="ExternalInput").ap()
    ones_d = nc.dram_tensor("ones512", [65, 512], F32, kind="ExternalInput").ap()
    eps_d = nc.dram_tensor("epscol", [128, 2], F32, kind="ExternalInput").ap()
    y_d = nc.dram_tensor("y", [T, D], F32, kind="ExternalOutput").ap()
    if probe:
        dbg_px = nc.dram_tensor("dbg_px", [T, D], F32, kind="ExternalOutput").ap()
        dbg_pp = nc.dram_tensor("dbg_pp", [128, 17408], BF16, kind="ExternalOutput").ap()
        dbg_pv = nc.dram_tensor("dbg_pv", [128, NT, D], BF16, kind="ExternalOutput").ap()
        dbg_pr = nc.dram_tensor("dbg_pr", [128, NT], F32, kind="ExternalOutput").ap()
        dbg_py = nc.dram_tensor("dbg_py", [T, D], F32, kind="ExternalOutput").ap()
        dbg_st0 = nc.dram_tensor("dbg_st0", [128, 28], F32, kind="ExternalOutput").ap()
        dbg_xt0 = nc.dram_tensor("dbg_xt0", [128, D], F32, kind="ExternalOutput").ap()
        dbg_st1 = nc.dram_tensor("dbg_st1", [128, 28], F32, kind="ExternalOutput").ap()
        dbg_arr = nc.dram_tensor("dbg_arr", [128, 96], F32, kind="ExternalOutput").ap()
    if debug:
        dbg_x1 = nc.dram_tensor("dbg_x1", [T, D], F32, kind="ExternalOutput").ap()
        dbg_gate = nc.dram_tensor("dbg_gate", [NO, 128, T], BF16, kind="ExternalOutput").ap()
        dbg_stat = nc.dram_tensor("dbg_stat", [128, 28], F32, kind="ExternalOutput").ap()
        dbg_q = nc.dram_tensor("dbg_q", [128, ND, T], BF16, kind="ExternalOutput").ap()
        dbg_k = nc.dram_tensor("dbg_k", [128, ND, T], BF16, kind="ExternalOutput").ap()
        dbg_v = nc.dram_tensor("dbg_v", [128, NT, D], BF16, kind="ExternalOutput").ap()
        dbg_p = nc.dram_tensor("dbg_p", [128, 17408], BF16, kind="ExternalOutput").ap()
        dbg_rcol = nc.dram_tensor("dbg_rcol", [128, NT], F32, kind="ExternalOutput").ap()
        dbg_xt = nc.dram_tensor("dbg_xt", [128, ND, T], BF16, kind="ExternalOutput").ap()
        dbg_l = nc.dram_tensor("dbg_l", [128, T], F32, kind="ExternalOutput").ap()
        dbg_ln = nc.dram_tensor("dbg_ln", [128, 3 * NT], F32, kind="ExternalOutput").ap()
        dbg_y2 = nc.dram_tensor("dbg_y2", [128, D], F32, kind="ExternalOutput").ap()
        dbg_ln2 = nc.dram_tensor("dbg_ln2", [128, 3 * NT], F32, kind="ExternalOutput").ap()
        dbg_wdh = nc.dram_tensor("dbg_wdh", [128, NO, 512], BF16, kind="ExternalOutput").ap()
        dbg_nm = nc.dram_tensor("dbg_nm", [128, 32], F32, kind="ExternalOutput").ap()

    agw_in = nc.dram_tensor("agw_in", [128, 2 * O], BF16)
    agw_out = nc.dram_tensor("agw_out", [D, 2 * O], BF16)
    agd_in = nc.dram_tensor("agd_in", [O // NC, D], BF16)
    agd_out = nc.dram_tensor("agd_out", [O, D], BF16)
    cc_in = [nc.dram_tensor(f"cc{i}_in", [1, 256], F32) for i in range(4)]
    cc_out = [nc.dram_tensor(f"cc{i}_out", [1, 256], F32) for i in range(4)]
    up_hbm = nc.dram_tensor("up_hbm", [NO, 128, T], BF16)

    ctx = contextlib.ExitStack()
    with ctx:
        sems = {}
        counts = {}
        for name in ("sL", "sG", "sP", "sS", "sV", "sC", "sO", "sA", "sB", "sK", "sGM"):
            sems[name] = ctx.enter_context(nc.semaphore(name))
            counts[name] = 0

        def inc(name, instr, k=1):
            instr.then_inc(sems[name], k)
            counts[name] += k
            return counts[name]

        def wv(eng, name, val):
            if val > 0:
                eng.wait_ge(sems[name], val)

        blk = ctx.enter_context(nc.Block())

        def barrier():
            snap = {k: v for k, v in counts.items() if k not in ("sC", "sO")}
            for engname in ("sync", "tensor", "scalar", "vector", "gpsimd"):
                @getattr(blk, engname)
                def _(eng):
                    for nm, v in snap.items():
                        if v > 0:
                            eng.wait_ge(sems[nm], v)

        maskt = ctx.enter_context(nc.sbuf_tensor("maskt", [128, 128], BF16, side="left"))
        identt = ctx.enter_context(nc.sbuf_tensor("identt", [128, 128], BF16, side="left"))
        onest = ctx.enter_context(nc.sbuf_tensor("onest", [65, 512], F32, side="left"))
        biast = ctx.enter_context(nc.sbuf_tensor("biast", [65, D], F32, side="left"))
        stat = ctx.enter_context(nc.sbuf_tensor("stat", [128, 28], F32, side="left"))
        rowbuf = ctx.enter_context(nc.sbuf_tensor("rowbuf", [1, 260], F32, side="left"))
        rcol = ctx.enter_context(nc.sbuf_tensor("rcol", [128, NT], F32, side="left"))


        (S_MS, S_SQ, S_M, S_VAR, S_INV, S_NM, S_SSUM, S_T0, S_T1, S_RJ,
         S_GMX, S_SX, S_RSX, S_GU, S_GG, S_SU, S_RSU, S_SG, S_RSG,
         S_G2M, S_SG2, S_RSG2, S_OM, S_SO, S_RSO) = range(25)

        S_JNK = 25

        def col(i):
            return stat[:, i:i + 1]

        def ssettle(scalar):
            scalar.copy(stat[:, S_JNK:S_JNK + 1], stat[:, 26:27])
            return scalar.copy(stat[:, S_JNK:S_JNK + 1], stat[:, 26:27])

        def vsettle(vector):
            vector.tensor_copy(stat[:, S_JNK:S_JNK + 1], stat[:, 26:27])
            return vector.tensor_copy(stat[:, S_JNK:S_JNK + 1], stat[:, 26:27])

        def vrecip(vector, out, in_):
            vector.tensor_copy(stat[:, S_JNK:S_JNK + 1], stat[:, 26:27])
            vector.reciprocal(out, in_)
            vector.tensor_copy(stat[:, S_JNK:S_JNK + 1], stat[:, 26:27])
            vector.tensor_copy(stat[:, S_JNK:S_JNK + 1], stat[:, 26:27])
            vector.reciprocal(out, in_)
            vector.tensor_copy(stat[:, S_JNK:S_JNK + 1], stat[:, 26:27])
            return vector.tensor_copy(stat[:, S_JNK:S_JNK + 1], stat[:, 26:27])

        def vsep8(vector):
            for _ in range(8):
                vector.tensor_copy(stat[:, S_JNK:S_JNK + 1], stat[:, 26:27])

        def vsep(vector):
            vector.tensor_copy(stat[:, S_JNK:S_JNK + 1], stat[:, 26:27])
            vector.tensor_copy(stat[:, S_JNK:S_JNK + 1], stat[:, 26:27])

        ps = [ctx.enter_context(nc.psum_tensor(f"ps{i}", [128, 512], F32))
              for i in range(6)]
        psT = [ctx.enter_context(nc.psum_tensor(f"psT{i}", [128, 128], BF16))
               for i in range(2)]

        ps_busy = {}

        def mm_groups(groups, ps_list, evac_engine, evac_fn, pre_wait=None):
            sem_e = "sS" if evac_engine == "scalar" else "sV"
            recs = []
            for g, emit in enumerate(groups):
                pbuf = ps_list[g % len(ps_list)]

                @blk.tensor
                def _(tensor, g=g, emit=emit, pbuf=pbuf):
                    if pre_wait is not None and g == 0:
                        for nm, v in pre_wait:
                            wv(tensor, nm, v)
                    busy = ps_busy.get(id(pbuf))
                    if busy is not None:
                        wv(tensor, busy[0], busy[1])
                    last = emit(tensor, pbuf)
                    inc("sP", last)
                pe_val = counts["sP"]

                @getattr(blk, evac_engine)
                def _(eng, g=g, pbuf=pbuf, pe_val=pe_val):
                    eng.wait_ge(sems["sP"], pe_val)
                    ins = evac_fn(eng, g, pbuf)
                    recs.append((sem_e, inc(sem_e, ins)))
                ps_busy[id(pbuf)] = recs[-1]
            return recs

        psT_busy = {0: 0, 1: 0}

        def transpose_seq(n, src_fn, dst_fn, first_wait=None):
            tr_vals = []
            ev_vals = []
            for i in range(n):
                @blk.tensor
                def _(tensor, i=i):
                    if i == 0 and first_wait:
                        for nm, v in first_wait:
                            wv(tensor, nm, v)
                    wv(tensor, "sS", psT_busy[i % 2])
                    ins = tensor.transpose(psT[i % 2][:, :], src_fn(i), identt[:, :])
                    tr_vals.append(inc("sP", ins))

                @blk.scalar
                def _(scalar, i=i):
                    scalar.wait_ge(sems["sP"], tr_vals[i])
                    ins = scalar.copy(dst_fn(i), psT[i % 2][:, :])
                    ev_vals.append(inc("sS", ins))
                psT_busy[i % 2] = ev_vals[-1]
            return ev_vals[-1]

        stage_busy = {}

        def load_cast(n, src_fn, dst_fn, stage, sync_first_wait=None):
            ld_vals = []
            cast_base = counts["sV"]
            keys = [stage[0].name, stage[1].name]

            @blk.sync
            def _(sync):
                if sync_first_wait:
                    for nm, v in sync_first_wait:
                        wv(sync, nm, v)
                for i in range(n):
                    if i >= 2:
                        sync.wait_ge(sems["sV"], cast_base + i - 1)
                    else:
                        wv(sync, "sV", stage_busy.get(keys[i % 2], 0))
                    pn = "sA" if i % 2 == 0 else "sB"
                    sync.dma_start(out=stage[i % 2], in_=src_fn(i)).then_inc(sems[pn], 16)
                    counts[pn] += 16
                    ld_vals.append((pn, counts[pn]))

            @blk.vector
            def _(vector):
                for i in range(n):
                    vector.wait_ge(sems[ld_vals[i][0]], ld_vals[i][1])
                    ins = vector.tensor_copy(dst_fn(i), stage[i % 2])
                    inc("sV", ins)
            for i in range(n):
                stage_busy[keys[i % 2]] = cast_base + i + 1
            return counts["sV"]

        gm_last_ind = [0]

        def global_max(idx, src_cols, dst_cols, wait_pre):
            k = len(src_cols)

            @blk.sync
            def _(sync):
                for nm, v in wait_pre:
                    wv(sync, nm, v)
                wv(sync, "sL", gm_last_ind[0])
                for j, sc in enumerate(src_cols):
                    sync.dma_start(out=rowbuf[0:1, 128 * j:128 * (j + 1)],
                                   in_=sc).then_inc(sems["sL"], 16)
                    counts["sL"] += 16
            ROWS = counts["sL"]

            @blk.vector
            def _(vector):
                vector.wait_ge(sems["sL"], ROWS)
                for j in range(k):
                    vector.reduce_max(rowbuf[0:1, 256 + j:257 + j],
                                      rowbuf[0:1, 128 * j:128 * (j + 1)], AX.X)
                vsep(vector)
                for j in range(k):
                    vector.tensor_scalar_mul(rowbuf[0:1, 128 * j:128 * (j + 1)],
                                             onest[0:1, 0:128],
                                             rowbuf[0:1, 256 + j:257 + j])
                inc("sV", vsettle(vector))
            REP = counts["sV"]

            @blk.sync
            def _(sync):
                sync.wait_ge(sems["sV"], REP)
                sync.dma_start(out=cc_in[idx].ap()[:, 0:128 * k],
                               in_=rowbuf[0:1, 0:128 * k]).then_inc(sems["sL"], 16)
                counts["sL"] += 16
            gm_last_ind[0] = counts["sL"]
            IND = counts["sL"]

            @blk.gpsimd
            def _(gpsimd):
                gpsimd.wait_ge(sems["sL"], IND)
                gpsimd.collective_compute(
                    "AllReduce", OP.max, replica_groups=RG,
                    ins=[cc_in[idx].ap()[:, 0:128 * k].opt()],
                    outs=[cc_out[idx].ap()[:, 0:128 * k].opt()],
                ).then_inc(sems["sC"], 1)
                counts["sC"] += 1
                gpsimd.wait_ge(sems["sC"], counts["sC"])
                for j, dc in enumerate(dst_cols):
                    gpsimd.dma_start(
                        out=dc,
                        in_=cc_out[idx].ap()[:, 128 * j:128 * (j + 1)].rearrange(
                            "a b -> b a")).then_inc(sems["sGM"], 16)
                    counts["sGM"] += 16
            return counts["sGM"]

        atn = contextlib.ExitStack()
        Vb = atn.enter_context(nc.sbuf_tensor("Vb", [128, NT, D], BF16, side="right"))
        POFF = []
        off = 0
        for jt in range(NT):
            POFF.append(off)
            off += T - 128 * jt
        PSZ = off
        Pb = atn.enter_context(nc.sbuf_tensor("Pb", [128, PSZ], BF16, side="right"))
        Lst = atn.enter_context(nc.sbuf_tensor("Lst", [128, T], F32, side="right"))
        nmlog = atn.enter_context(nc.sbuf_tensor("nmlog", [128, 2 * NT], F32, side="right")) if debug else None
        lnlog = atn.enter_context(nc.sbuf_tensor("lnlog", [128, 3 * NT], F32, side="right")) if debug else None
        qk_s = contextlib.ExitStack()
        QTb = qk_s.enter_context(nc.sbuf_tensor("QTb", [128, ND, T], BF16, side="right"))
        KTb = qk_s.enter_context(nc.sbuf_tensor("KTb", [128, ND, T], BF16, side="right"))

        dq_s = contextlib.ExitStack()
        wsl = dq_s.enter_context(nc.sbuf_tensor("wsl", [128, O], F32, side="right"))
        wqs = dq_s.enter_context(nc.sbuf_tensor("wqs", [128, O], BF16, side="right"))
        wtr = dq_s.enter_context(nc.sbuf_tensor("wtr", [128, O], BF16, side="right"))
        dstat = dq_s.enter_context(nc.sbuf_tensor("dstat", [128, 192], F32, side="right"))

        @blk.sync
        def _(sync):
            sync.dma_start(out=maskt[:, :], in_=mask_d[:, :]).then_inc(sems["sK"], 16)
            sync.dma_start(out=identt[:, :], in_=ident_d[:, :]).then_inc(sems["sK"], 16)
            sync.dma_start(out=onest[:, :], in_=ones_d[:, :]).then_inc(sems["sK"], 16)
            sync.dma_start(out=biast[:, :], in_=biases_d[:, :]).then_inc(sems["sK"], 16)
            sync.dma_start(out=stat[:, 26:28], in_=eps_d[:, :]).then_inc(sems["sK"], 16)
            counts["sK"] += 80
            sync.dma_start(out=wsl[:].rearrange("p (a b) -> p a b", b=128),
                           in_=wup_d.rearrange("(a p) b -> p a b", p=128)).then_inc(sems["sL"], 16)
            counts["sL"] += 16
        LD_WUP = counts["sL"]

        def emit_dequant(ld_val, pre=None):
            @blk.vector
            def _(vector):
                if pre is not None:
                    wv(vector, pre[0], pre[1])
                vector.wait_ge(sems["sL"], ld_val)
                w3 = wsl[:].rearrange("p (a k) -> p a k", k=32)
                bmax = dstat[:, 0:96]
                s_ = dstat[:, 96:192]
                vector.tensor_reduce(bmax, w3, AX.X, OP.max, apply_absolute_value=True)
                vsep(vector)
                vector.tensor_scalar(s_, bmax, WQMAX_INV, 1e-12, OP.mult, OP.max)
                vsep(vector)
                vrecip(vector, bmax, s_)
                vsep(vector)
                rs3 = bmax.unsqueeze(2).broadcast_to([128, 96, 32])
                s3 = s_.unsqueeze(2).broadcast_to([128, 96, 32])
                vector.tensor_tensor(w3, w3, rs3, OP.mult)
                vector.tensor_scalar(w3, w3, CR, CR, OP.add, OP.subtract)
                o3 = wqs[:].rearrange("p (a k) -> p a k", k=32)
                ins = vector.tensor_tensor(o3, w3, s3, OP.mult)
                inc("sV", ins)
            return counts["sV"]

        wqs24 = wqs[:].rearrange("p (a b) -> p a b", b=128)
        DQ_UP = emit_dequant(LD_WUP)
        TR_UP = transpose_seq(
            NO, lambda i: wqs24[:, i, :],
            lambda i: wtr[:, 128 * i:128 * (i + 1)],
            first_wait=[("sV", DQ_UP), ("sK", 80)])

        @blk.gpsimd
        def _(gpsimd):
            gpsimd.wait_ge(sems["sS"], TR_UP)
            gpsimd.dma_start(out=agw_in.ap()[:, 0:O], in_=wtr[:, :]).then_inc(sems["sG"], 16)
            counts["sG"] += 16
        WUP_D = counts["sG"]

        @blk.sync
        def _(sync):
            sync.wait_ge(sems["sV"], DQ_UP)
            sync.dma_start(out=wsl[:].rearrange("p (a b) -> p a b", b=128),
                           in_=wgate_d.rearrange("(a p) b -> p a b", p=128)).then_inc(sems["sL"], 16)
            counts["sL"] += 16
        LD_WGATE = counts["sL"]

        DQ_GATE = emit_dequant(LD_WGATE, pre=("sG", WUP_D))
        TR_GATE = transpose_seq(
            NO, lambda i: wqs24[:, i, :],
            lambda i: wtr[:, 128 * i:128 * (i + 1)],
            first_wait=[("sV", DQ_GATE), ("sG", WUP_D)])

        @blk.gpsimd
        def _(gpsimd):
            gpsimd.wait_ge(sems["sS"], TR_GATE)
            gpsimd.dma_start(out=agw_in.ap()[:, O:2 * O], in_=wtr[:, :]).then_inc(sems["sG"], 16)
            counts["sG"] += 16
            gpsimd.wait_ge(sems["sG"], counts["sG"])
            gpsimd.collective_compute(
                "AllGather", OP.bypass, replica_groups=RG,
                ins=[agw_in.ap().opt()], outs=[agw_out.ap().opt()],
            ).then_inc(sems["sC"], 1)
            counts["sC"] += 1
        WGATE_D = counts["sG"]
        AGW_DONE = counts["sC"]

        @blk.sync
        def _(sync):
            sync.wait_ge(sems["sV"], DQ_GATE)
            sync.dma_start(out=wsl[:].rearrange("p (a b) -> p a b", b=384),
                           in_=wdown_d.rearrange("(a p) b -> p a b", p=128)).then_inc(sems["sL"], 16)
            counts["sL"] += 16
        LD_WDOWN = counts["sL"]

        DQ_DOWN = emit_dequant(LD_WDOWN, pre=("sG", WGATE_D))
        wqs_d3 = wqs[:].rearrange("p (a b) -> p a b", b=384)
        wtr_d3 = wtr[:].rearrange("p (a b) -> p a b", b=1024)

        def _src_dn(i):
            ht, oc = divmod(i, 3)
            return wqs_d3[:, ht, 128 * oc:128 * (oc + 1)]

        def _dst_dn(i):
            ht, oc = divmod(i, 3)
            return wtr_d3[:, oc, 128 * ht:128 * (ht + 1)]

        TR_DOWN = transpose_seq(NO, _src_dn, _dst_dn, first_wait=[("sV", DQ_DOWN)])

        @blk.gpsimd
        def _(gpsimd):
            gpsimd.wait_ge(sems["sS"], TR_DOWN)
            gpsimd.dma_start(out=agd_in.ap().rearrange("(a p) b -> p a b", p=128),
                             in_=wtr_d3).then_inc(sems["sG"], 16)
            counts["sG"] += 16
            gpsimd.wait_ge(sems["sG"], counts["sG"])
            gpsimd.collective_compute(
                "AllGather", OP.bypass, replica_groups=RG,
                ins=[agd_in.ap().opt()], outs=[agd_out.ap().opt()],
            ).then_inc(sems["sC"], 1)
            counts["sC"] += 1
        AGD_DONE = counts["sC"]

        barrier()
        dq_s.close()

        a1 = contextlib.ExitStack()
        xTb = a1.enter_context(nc.sbuf_tensor("xTb", [128, ND, T], BF16, side="right"))
        wstage = [a1.enter_context(nc.sbuf_tensor(f"wst{i}", [128, D], F32, side="right"))
                  for i in range(2)]
        wbf = a1.enter_context(nc.sbuf_tensor("wbf", [128, ND, D], BF16, side="right"))

        XT_CAST = load_cast(
            2 * ND,
            lambda i: xT_d[128 * (i // 2):128 * (i // 2 + 1), 1024 * (i % 2):1024 * (i % 2 + 1)],
            lambda i: xTb[:, i // 2, 1024 * (i % 2):1024 * (i % 2 + 1)],
            [wstage[0][:, :], wstage[1][:, :]])

        WQ_CAST = load_cast(
            ND, lambda i: wq_d[128 * i:128 * (i + 1), :],
            lambda i: wbf[:, i, :], [wstage[0][:, :], wstage[1][:, :]])

        def qk_group(bias_row, kt, ic):
            def emit(tensor, psum):
                for dt in range(ND):
                    tensor.matmul(psum[:, :], wbf[:, dt, 128 * kt:128 * (kt + 1)],
                                  xTb[:, dt, 512 * ic:512 * (ic + 1)],
                                  start=(dt == 0), stop=False)
                return tensor.matmul(
                    psum[:, :],
                    biast[32 * bias_row:32 * bias_row + 1, 128 * kt:128 * (kt + 1)],
                    onest[32 * bias_row:32 * bias_row + 1, :], start=False, stop=True)
            return emit

        if debug:
            @blk.sync
            def _(sync):
                sync.wait_ge(sems["sV"], XT_CAST)
                sync.dma_start(out=dbg_xt[:, :, :], in_=xTb[:, :, :]).then_inc(sems["sO"], 16)
                counts["sO"] += 16

        qg = [(kt, ic) for kt in range(ND) for ic in range(4)]
        mm_groups([qk_group(0, kt, ic) for kt, ic in qg], ps[0:3], "scalar",
                  lambda eng, g, psum: eng.copy(
                      QTb[:, qg[g][0], 512 * qg[g][1]:512 * (qg[g][1] + 1)], psum[:, :]),
                  pre_wait=[("sV", WQ_CAST), ("sV", XT_CAST)])
        Q_MM = counts["sP"]

        WK_CAST = load_cast(
            ND, lambda i: wk_d[128 * i:128 * (i + 1), :],
            lambda i: wbf[:, i, :], [wstage[0][:, :], wstage[1][:, :]],
            sync_first_wait=[("sP", Q_MM)])

        mm_groups([qk_group(1, kt, ic) for kt, ic in qg], ps[0:3], "scalar",
                  lambda eng, g, psum: eng.copy(
                      KTb[:, qg[g][0], 512 * qg[g][1]:512 * (qg[g][1] + 1)], psum[:, :]),
                  pre_wait=[("sV", WK_CAST)])
        K_MM = counts["sP"]

        WV_CAST = load_cast(
            ND, lambda i: wv_d[128 * i:128 * (i + 1), :],
            lambda i: wbf[:, i, :], [wstage[0][:, :], wstage[1][:, :]],
            sync_first_wait=[("sP", K_MM)])

        def v_group(jt, vc):
            def emit(tensor, psum):
                for dt in range(ND):
                    tensor.matmul(psum[:, :], xTb[:, dt, 128 * jt:128 * (jt + 1)],
                                  wbf[:, dt, 512 * vc:512 * (vc + 1)],
                                  start=(dt == 0), stop=False)
                return tensor.matmul(
                    psum[:, :], onest[64:65, 0:128],
                    biast[64:65, 512 * vc:512 * (vc + 1)],
                    start=False, stop=True)
            return emit

        vg = [(jt, vc) for jt in range(NT) for vc in range(2)]
        mm_groups([v_group(jt, vc) for jt, vc in vg], ps[0:3], "scalar",
                  lambda eng, g, psum: eng.copy(
                      Vb[:, vg[g][0], 512 * vg[g][1]:512 * (vg[g][1] + 1)], psum[:, :]),
                  pre_wait=[("sV", WV_CAST)])

        barrier()
        a1.close()

        jt_exp = {}
        for jt in range(NT):
            Li = T - 128 * jt
            nch = (Li + 511) // 512

            def l_group(jt, c):
                def emit(tensor, psum):
                    i0 = 128 * jt + 512 * c
                    n = min(512, T - i0)
                    last = None
                    for kt in range(ND):
                        last = tensor.matmul(
                            psum[:, 0:n], KTb[:, kt, 128 * jt:128 * (jt + 1)],
                            QTb[:, kt, i0:i0 + n],
                            start=(kt == 0), stop=(kt == ND - 1))
                    return last
                return emit

            def mk_evac(jt):
                def evac(eng, g, psum):
                    i0r = 512 * g
                    n = min(512, (T - 128 * jt) - i0r)
                    return eng.copy(Lst[:, i0r:i0r + n], psum[:, 0:n])
                return evac

            mm_groups([l_group(jt, c) for c in range(nch)], ps[0:3], "scalar",
                      mk_evac(jt))
            LJT = counts["sS"]

            @blk.vector
            def _(vector, jt=jt, Li=Li, LJT=LJT):
                vector.wait_ge(sems["sS"], LJT)
                vector.tensor_tensor(Lst[:, 0:128], Lst[:, 0:128], maskt[:, :], OP.add)
                vector.tensor_reduce(col(S_NM), Lst[:, 0:Li], AX.X, OP.max,
                                     negate=True)
                vector.tensor_copy(stat[:, S_JNK:S_JNK + 1], stat[:, 26:27])
                ins = vector.tensor_copy(stat[:, S_JNK:S_JNK + 1], stat[:, 26:27])
                inc("sV", ins)
            NM = counts["sV"]

            @blk.scalar
            def _(scalar, jt=jt, Li=Li, NM=NM):
                scalar.wait_ge(sems["sV"], NM)
                scalar.activation(Pb[:, POFF[jt]:POFF[jt] + Li], Lst[:, 0:Li],
                                  ACT.Exp, bias=col(S_NM), scale=1.0,
                                  accum_out=col(S_SSUM))
                scalar.copy(stat[:, S_JNK:S_JNK + 1], stat[:, 26:27])
                ins = scalar.copy(stat[:, S_JNK:S_JNK + 1], stat[:, 26:27])
                jt_exp[jt] = inc("sS", ins)

            @blk.vector
            def _(vector, jt=jt):
                vector.wait_ge(sems["sS"], jt_exp[jt])
                if debug:
                    vector.tensor_copy(nmlog[:, jt:jt + 1], col(S_NM))
                    vector.tensor_copy(nmlog[:, NT + jt:NT + jt + 1], col(S_SSUM))
                ins = vector.tensor_copy(rcol[:, jt:jt + 1], col(S_SSUM))
                inc("sV", ins)

        if debug:
            LAST_EXP = jt_exp[NT - 1]
            @blk.sync
            def _(sync):
                sync.wait_ge(sems["sS"], LAST_EXP)
                sync.dma_start(out=dbg_l[:, :], in_=Lst[:, :]).then_inc(sems["sO"], 16)
                sync.dma_start(out=dbg_nm[:, :], in_=nmlog[:, :]).then_inc(sems["sO"], 16)
                counts["sO"] += 32

        if debug:
            A2_DONE_V = counts["sV"]
            @blk.sync
            def _(sync):
                sync.wait_ge(sems["sV"], A2_DONE_V)
                sync.dma_start(out=dbg_q[:, :, :], in_=QTb[:, :, :]).then_inc(sems["sO"], 16)
                sync.dma_start(out=dbg_k[:, :, :], in_=KTb[:, :, :]).then_inc(sems["sO"], 16)
                sync.dma_start(out=dbg_v[:, :, :], in_=Vb[:, :, :]).then_inc(sems["sO"], 16)
                sync.dma_start(out=dbg_p[:, :], in_=Pb[:, :]).then_inc(sems["sO"], 16)
                sync.dma_start(out=dbg_rcol[:, :], in_=rcol[:, :]).then_inc(sems["sO"], 16)
                counts["sO"] += 80

        @blk.vector
        def _(vector):
            t32 = Lst[:, 1024:1024 + NT]
            vsep(vector)
            vector.tensor_scalar_mul(t32, rcol[:, :], 32.0)
            vsep8(vector)
            vrecip(vector, rcol[:, :], t32)
            vsep8(vector)
            ins = None
            for jt in range(NT):
                ins = vector.tensor_scalar_mul(Vb[:, jt, :], Vb[:, jt, :],
                                               rcol[:, jt:jt + 1])
            inc("sV", ins)

        barrier()
        if probe:
            @blk.sync
            def _(sync):
                sync.dma_start(out=dbg_pp[:, :], in_=Pb[:, :]).then_inc(sems["sO"], 16)
                sync.dma_start(out=dbg_pv[:, :, :], in_=Vb[:, :, :]).then_inc(sems["sO"], 16)
                sync.dma_start(out=dbg_pr[:, :], in_=rcol[:, :]).then_inc(sems["sO"], 16)
                counts["sO"] += 48
        qk_s.close()

        x1 = ctx.enter_context(nc.sbuf_tensor("x1", [128, NT, D], F32, side="left"))

        x_ld = []
        @blk.sync
        def _(sync):
            for it in range(NT):
                sync.dma_start(out=x1[:, it, :],
                               in_=x_d[128 * it:128 * (it + 1), :]).then_inc(sems["sL"], 16)
                counts["sL"] += 16
                x_ld.append(counts["sL"])

        pvg = [(it, vh) for it in range(NT) for vh in range(2)]

        def pv_group(it, vh):
            def emit(tensor, psum):
                last = None
                for jt in range(it + 1):
                    lhs = Pb[:, POFF[jt] + 128 * (it - jt):POFF[jt] + 128 * (it - jt) + 128]
                    last = tensor.matmul(psum[:, :], lhs,
                                         Vb[:, jt, 512 * vh:512 * (vh + 1)],
                                         start=(jt == 0), stop=(jt == it))
                return last
            return emit

        def pv_evac(eng, g, psum):
            it, vh = pvg[g]
            eng.wait_ge(sems["sL"], x_ld[-1])
            sl = x1[:, it, 512 * vh:512 * (vh + 1)]
            return eng.scalar_tensor_tensor(sl, sl, 2.0, psum[:, :], OP.mult, OP.add)

        pv_recs = mm_groups([pv_group(it, vh) for it, vh in pvg], ps[0:4], "vector",
                            pv_evac)

        PROBE_Y = [0]
        if probe:
            @blk.sync
            def _(sync):
                sync.wait_ge(sems["sV"], pv_recs[-1][1])
                for it in range(NT):
                    sync.dma_start(out=dbg_py[128 * it:128 * (it + 1), :],
                                   in_=x1[:, it, :]).then_inc(sems["sO"], 16)
                    counts["sO"] += 16
            PROBE_Y[0] = counts["sO"]

        MS16 = Lst[:, 1056:1056 + NT]
        SQ16 = Lst[:, 1056 + NT:1056 + 2 * NT]
        M16 = Lst[:, 1056 + 2 * NT:1056 + 3 * NT]
        V16 = Lst[:, 1056 + 3 * NT:1056 + 4 * NT]
        SD16 = Lst[:, 1056 + 4 * NT:1056 + 5 * NT]
        I16 = Lst[:, 1056 + 5 * NT:1056 + 6 * NT]

        for it in range(NT):
            y_done = pv_recs[2 * it + 1][1]

            @blk.vector
            def _(vector, it=it, y_done=y_done):
                vector.wait_ge(sems["sV"], y_done)
                vector.reduce_sum(col(S_MS), x1[:, it, :], AX.X)
                vector.tensor_mul(Lst[:, 0:D], x1[:, it, :], x1[:, it, :])
                vector.tensor_copy(MS16[:, it:it + 1], col(S_MS))
                ins = vector.reduce_sum(SQ16[:, it:it + 1], Lst[:, 0:D], AX.X)
                inc("sV", ins)

        @blk.vector
        def _(vector):
            vsep8(vector)
            vector.tensor_scalar_mul(M16, MS16, 1.0 / D)
            vector.tensor_scalar_mul(V16, SQ16, 1.0 / D)
            vsep(vector)
            vector.tensor_mul(SD16, M16, M16)
            vsep(vector)
            vector.tensor_sub(V16, V16, SD16)
            inc("sV", vsettle(vector))
        VARB = counts["sV"]

        @blk.scalar
        def _(scalar):
            scalar.wait_ge(sems["sV"], VARB)
            scalar.activation(SD16, V16, ACT.Sqrt, bias=stat[:, 26:27], scale=1.0)
            inc("sS", ssettle(scalar))
        STDB = counts["sS"]

        @blk.vector
        def _(vector):
            vector.wait_ge(sems["sS"], STDB)
            vrecip(vector, I16, SD16)
            vsep8(vector)
            for it in range(NT):
                vector.tensor_scalar(x1[:, it, :], x1[:, it, :],
                                     M16[:, it:it + 1], I16[:, it:it + 1],
                                     OP.subtract, OP.mult)
                vector.tensor_reduce(col(S_T0), x1[:, it, :], AX.X, OP.max,
                                     apply_absolute_value=True)
                vsep(vector)
                if it == 0:
                    vector.tensor_copy(col(S_GMX), col(S_T0))
                else:
                    vector.tensor_max(col(S_GMX), col(S_GMX), col(S_T0))
            inc("sV", vsettle(vector))
        ln1_last = counts["sV"]

        if debug:
            @blk.sync
            def _(sync):
                sync.wait_ge(sems["sV"], ln1_last)
                for it in range(NT):
                    sync.dma_start(out=dbg_x1[128 * it:128 * (it + 1), :],
                                   in_=x1[:, it, :]).then_inc(sems["sO"], 16)
                    counts["sO"] += 16

        barrier()
        if probe:
            @blk.sync
            def _(sync):
                for it in range(NT):
                    sync.dma_start(out=dbg_px[128 * it:128 * (it + 1), :],
                                   in_=x1[:, it, :]).then_inc(sems["sO"], 16)
                    counts["sO"] += 16
                sync.dma_start(out=dbg_arr[:, :], in_=Lst[:, 1056:1152]).then_inc(sems["sO"], 16)
                counts["sO"] += 16
        atn.close()

        ffn = contextlib.ExitStack()
        gateb = ffn.enter_context(nc.sbuf_tensor("gateb", [128, NO, T], BF16, side="right"))
        ffn_xq = contextlib.ExitStack()
        xqT = ffn_xq.enter_context(nc.sbuf_tensor("xqT", [128, ND, T], BF16, side="right"))
        a4_s = contextlib.ExitStack()
        fscr = a4_s.enter_context(nc.sbuf_tensor("fscr", [128, D], F32, side="right"))
        xqst = a4_s.enter_context(nc.sbuf_tensor("xqst", [128, D], BF16, side="right"))

        GM1 = global_max(0, [col(S_GMX)], [col(S_GMX)], [("sV", ln1_last)])

        @blk.vector
        def _(vector):
            vector.wait_ge(sems["sGM"], GM1)
            vector.tensor_scalar(col(S_SX), col(S_GMX), INV_QMAX, 1e-12, OP.mult, OP.max)
            vsep(vector)
            vrecip(vector, col(S_RSX), col(S_SX))
            vector.tensor_copy(stat[:, S_JNK:S_JNK + 1], stat[:, 26:27])
            ins = vector.tensor_copy(stat[:, S_JNK:S_JNK + 1], stat[:, 26:27])
            inc("sV", ins)
        SXD = counts["sV"]

        xq_v = {}
        xq_ev = {-1: 0}
        for it in range(NT):
            @blk.scalar
            def _(scalar, it=it):
                if it == 0:
                    scalar.wait_ge(sems["sV"], SXD)
                else:
                    scalar.wait_ge(sems["sV"], xq_v[it - 1])
                ins = scalar.activation(fscr[:, :], x1[:, it, :], ACT.Identity,
                                        bias=stat[:, 27:28], scale=col(S_RSX))
                inc("sS", ins)
            TQ = counts["sS"]

            @blk.vector
            def _(vector, it=it, TQ=TQ):
                vector.wait_ge(sems["sS"], TQ)
                if it >= 1:
                    vector.wait_ge(sems["sS"], xq_ev[it - 1])
                ins = vector.tensor_scalar(xqst[:, :], fscr[:, :],
                                           CR, col(S_SX), OP.subtract, OP.mult)
                xq_v[it] = inc("sV", ins)

            xq_ev[it] = transpose_seq(
                ND,
                lambda kt: xqst[:, 128 * kt:128 * (kt + 1)],
                lambda kt, it=it: xqT[:, kt, 128 * it:128 * (it + 1)],
                first_wait=[("sV", xq_v[it])])

        barrier()
        a4_s.close()

        b_s = contextlib.ExitStack()
        wtile = [b_s.enter_context(nc.sbuf_tensor(f"wtile{i}", [128, ND, 128], BF16, side="right"))
                 for i in range(2)]
        upst = b_s.enter_context(nc.sbuf_tensor("upst", [128, T], BF16, side="right"))

        border = [(w_i, ot) for ot in range(NO) for w_i in range(2)]
        SP_B_BASE = counts["sP"]
        B_PE = [SP_B_BASE + 4 * (j + 1) for j in range(2 * NO)]
        wt_ld = []

        @blk.sync
        def _(sync):
            sync.wait_ge(sems["sC"], AGW_DONE)
            for j, (w_i, ot) in enumerate(border):
                if j >= 2:
                    sync.wait_ge(sems["sP"], B_PE[j - 2])
                ci = w_i * NO + ot
                pn = "sA" if j % 2 == 0 else "sB"
                sync.dma_start(
                    out=wtile[j % 2][:, :, :],
                    in_=agw_out.ap()[:, 128 * ci:128 * (ci + 1)].rearrange(
                        "(ht p) o -> p ht o", p=128)).then_inc(sems[pn], 16)
                counts[pn] += 16
                wt_ld.append((pn, counts[pn]))

        up_dma = {}
        up_max = {}
        gmax_seed = [False, False]
        for j, (w_i, ot) in enumerate(border):
            def b_group(j, ic):
                def emit(tensor, psum):
                    last = None
                    for ht in range(ND):
                        last = tensor.matmul(psum[:, :], wtile[j % 2][:, ht, :],
                                             xqT[:, ht, 512 * ic:512 * (ic + 1)],
                                             start=(ht == 0), stop=(ht == ND - 1))
                    return last
                return emit

            groups = [b_group(j, ic) for ic in range(4)]

            if w_i == 0:
                def mk_ev_up(ot):
                    def evac(eng, g, psum):
                        if g == 0 and ot >= 1:
                            eng.wait_ge(sems["sG"], up_dma[ot - 1])
                            eng.wait_ge(sems["sV"], up_max[ot - 1])
                        return eng.copy(upst[:, 512 * g:512 * (g + 1)], psum[:, :])
                    return evac
                recs = mm_groups(groups, ps[0:3], "scalar", mk_ev_up(ot),
                                 pre_wait=[wt_ld[j]])
                EV = recs[-1][1]

                @blk.gpsimd
                def _(gpsimd, ot=ot, EV=EV):
                    gpsimd.wait_ge(sems["sS"], EV)
                    gpsimd.dma_start(out=up_hbm.ap()[ot, :, :],
                                     in_=upst[:, :]).then_inc(sems["sG"], 16)
                    counts["sG"] += 16
                    up_dma[ot] = counts["sG"]

                @blk.vector
                def _(vector, ot=ot, EV=EV):
                    vector.wait_ge(sems["sS"], EV)
                    vector.tensor_reduce(col(S_T0), upst[:, :], AX.X, OP.max,
                                         apply_absolute_value=True)
                    vsep(vector)
                    if not gmax_seed[0]:
                        vector.tensor_copy(col(S_GU), col(S_T0))
                        gmax_seed[0] = True
                    else:
                        vector.tensor_max(col(S_GU), col(S_GU), col(S_T0))
                    up_max[ot] = inc("sV", vsettle(vector))
            else:
                def mk_ev_g(ot):
                    def evac(eng, g, psum):
                        return eng.copy(gateb[:, ot, 512 * g:512 * (g + 1)], psum[:, :])
                    return evac
                recs = mm_groups(groups, ps[0:3], "scalar", mk_ev_g(ot),
                                 pre_wait=[wt_ld[j]])
                EV = recs[-1][1]

                @blk.vector
                def _(vector, ot=ot, EV=EV):
                    vector.wait_ge(sems["sS"], EV)
                    vector.tensor_reduce(col(S_T0), gateb[:, ot, :], AX.X, OP.max,
                                         apply_absolute_value=True)
                    vsep(vector)
                    if not gmax_seed[1]:
                        vector.tensor_copy(col(S_GG), col(S_T0))
                        gmax_seed[1] = True
                    else:
                        vector.tensor_max(col(S_GG), col(S_GG), col(S_T0))
                    inc("sV", vsettle(vector))
        B_V_LAST = counts["sV"]

        barrier()
        b_s.close()
        ffn_xq.close()

        GM2 = global_max(1, [col(S_GU), col(S_GG)], [col(S_GU), col(S_GG)],
                         [("sV", B_V_LAST)])

        @blk.vector
        def _(vector):
            vector.wait_ge(sems["sGM"], GM2)
            vector.tensor_scalar(col(S_SU), col(S_GU), INV_QMAX, 1e-12, OP.mult, OP.max)
            vector.tensor_scalar(col(S_SG), col(S_GG), INV_QMAX, 1e-12, OP.mult, OP.max)
            vsep(vector)
            vrecip(vector, col(S_RSU), col(S_SU))
            vsep(vector)
            vrecip(vector, col(S_RSG), col(S_SG))
            vector.tensor_copy(stat[:, S_JNK:S_JNK + 1], stat[:, 26:27])
            ins = vector.tensor_copy(stat[:, S_JNK:S_JNK + 1], stat[:, 26:27])
            inc("sV", ins)
        SC2 = counts["sV"]

        c_s = contextlib.ExitStack()
        scrA = [c_s.enter_context(nc.sbuf_tensor(f"scrA{i}", [128, T], F32, side="right")) for i in range(2)]
        if debug:
            _scrB0 = c_s.enter_context(nc.sbuf_tensor("scrB0", [128, T], F32, side="right"))
            scrB = [_scrB0, _scrB0]
        else:
            scrB = [c_s.enter_context(nc.sbuf_tensor(f"scrB{i}", [128, T], F32, side="right")) for i in range(2)]
        upld = [c_s.enter_context(nc.sbuf_tensor(f"upld{i}", [128, T], BF16, side="right")) for i in range(2)]

        c1_v = {}
        for ot in range(NO):
            A = scrA[ot % 2]
            Bs = scrB[ot % 2]

            @blk.scalar
            def _(scalar, ot=ot, A=A):
                if ot == 0:
                    scalar.wait_ge(sems["sV"], SC2)
                if ot >= 2:
                    scalar.wait_ge(sems["sV"], c1_v[ot - 2])
                ins = scalar.activation(A[:, :], gateb[:, ot, :], ACT.Identity,
                                        bias=stat[:, 27:28], scale=col(S_RSG))
                inc("sS", ins)
            T1 = counts["sS"]

            @blk.vector
            def _(vector, ot=ot, A=A, T1=T1):
                vector.wait_ge(sems["sS"], T1)
                ins = vector.tensor_scalar(gateb[:, ot, :], A[:, :], CR, col(S_SG),
                                           OP.subtract, OP.mult)
                inc("sV", ins)
            GQ = counts["sV"]

            @blk.scalar
            def _(scalar, ot=ot, A=A, Bs=Bs, GQ=GQ):
                scalar.wait_ge(sems["sV"], GQ)
                scalar.activation(A[:, :], gateb[:, ot, :], ACT.Sigmoid)
                ins = scalar.activation(Bs[:, :], A[:, :], ACT.Identity,
                                        bias=stat[:, 27:28], scale=65536.0)
                inc("sS", ins)
            SGD = counts["sS"]

            @blk.vector
            def _(vector, ot=ot, Bs=Bs, SGD=SGD):
                vector.wait_ge(sems["sS"], SGD)
                vector.tensor_scalar(Bs[:, :], Bs[:, :], CR, 65535.0, OP.subtract, OP.min)
                vector.scalar_tensor_tensor(gateb[:, ot, :], gateb[:, ot, :],
                                            1.0 / 65536.0, Bs[:, :], OP.mult, OP.mult)
                vector.tensor_reduce(col(S_T0), gateb[:, ot, :], AX.X, OP.max,
                                     apply_absolute_value=True)
                vsep(vector)
                if ot == 0:
                    vector.tensor_copy(col(S_G2M), col(S_T0))
                else:
                    vector.tensor_max(col(S_G2M), col(S_G2M), col(S_T0))
                c1_v[ot] = inc("sV", vsettle(vector))
        C1_LAST = counts["sV"]

        GM3 = global_max(2, [col(S_G2M)], [col(S_G2M)], [("sV", C1_LAST)])

        @blk.vector
        def _(vector):
            vector.wait_ge(sems["sGM"], GM3)
            vector.tensor_scalar(col(S_SG2), col(S_G2M), INV_QMAX, 1e-12, OP.mult, OP.max)
            vsep(vector)
            vrecip(vector, col(S_RSG2), col(S_SG2))
            vector.tensor_copy(stat[:, S_JNK:S_JNK + 1], stat[:, 26:27])
            ins = vector.tensor_copy(stat[:, S_JNK:S_JNK + 1], stat[:, 26:27])
            inc("sV", ins)
        SC3 = counts["sV"]

        c2_v = {}
        up_rld = {}
        for ot in range(NO):
            A = scrA[ot % 2]
            Bs = scrB[ot % 2]

            @blk.gpsimd
            def _(gpsimd, ot=ot):
                if ot >= 2:
                    gpsimd.wait_ge(sems["sV"], c2_v[ot - 2])
                pn = "sA" if ot % 2 == 0 else "sB"
                gpsimd.dma_start(out=upld[ot % 2][:, :],
                                 in_=up_hbm.ap()[ot, :, :]).then_inc(sems[pn], 16)
                counts[pn] += 16
                up_rld[ot] = (pn, counts[pn])

            @blk.scalar
            def _(scalar, ot=ot, A=A, Bs=Bs):
                if ot == 0:
                    scalar.wait_ge(sems["sV"], SC3)
                if ot >= 2:
                    scalar.wait_ge(sems["sV"], c2_v[ot - 2])
                scalar.activation(A[:, :], gateb[:, ot, :], ACT.Identity,
                                  bias=stat[:, 27:28], scale=col(S_RSG2))
                scalar.wait_ge(sems[up_rld[ot][0]], up_rld[ot][1])
                ins = scalar.activation(Bs[:, :], upld[ot % 2][:, :], ACT.Identity,
                                        bias=stat[:, 27:28], scale=col(S_RSU))
                inc("sS", ins)
            TS = counts["sS"]

            @blk.vector
            def _(vector, ot=ot, A=A, Bs=Bs, TS=TS):
                vector.wait_ge(sems["sS"], TS)
                vector.tensor_scalar(gateb[:, ot, :], A[:, :], CR, col(S_SG2),
                                     OP.subtract, OP.mult)
                vector.tensor_scalar(upld[ot % 2][:, :], Bs[:, :], CR, col(S_SU),
                                     OP.subtract, OP.mult)
                vector.tensor_mul(gateb[:, ot, :], gateb[:, ot, :], upld[ot % 2][:, :])
                vector.tensor_reduce(col(S_T0), gateb[:, ot, :], AX.X, OP.max,
                                     apply_absolute_value=True)
                vsep(vector)
                if ot == 0:
                    vector.tensor_copy(col(S_OM), col(S_T0))
                else:
                    vector.tensor_max(col(S_OM), col(S_OM), col(S_T0))
                c2_v[ot] = inc("sV", vsettle(vector))
        C2_LAST = counts["sV"]

        GM4 = global_max(3, [col(S_OM)], [col(S_OM)], [("sV", C2_LAST)])

        @blk.vector
        def _(vector):
            vector.wait_ge(sems["sGM"], GM4)
            vector.tensor_scalar(col(S_SO), col(S_OM), INV_QMAX, 1e-12, OP.mult, OP.max)
            vsep(vector)
            vrecip(vector, col(S_RSO), col(S_SO))
            vector.tensor_copy(stat[:, S_JNK:S_JNK + 1], stat[:, 26:27])
            ins = vector.tensor_copy(stat[:, S_JNK:S_JNK + 1], stat[:, 26:27])
            inc("sV", ins)
        SC4 = counts["sV"]

        o_v = {}
        for ot in range(NO):
            A = scrA[ot % 2]

            @blk.scalar
            def _(scalar, ot=ot, A=A):
                if ot == 0:
                    scalar.wait_ge(sems["sV"], SC4)
                if ot >= 2:
                    scalar.wait_ge(sems["sV"], o_v[ot - 2])
                ins = scalar.activation(A[:, :], gateb[:, ot, :], ACT.Identity,
                                        bias=stat[:, 27:28], scale=col(S_RSO))
                inc("sS", ins)
            TO = counts["sS"]

            @blk.vector
            def _(vector, ot=ot, A=A, TO=TO):
                vector.wait_ge(sems["sS"], TO)
                ins = vector.tensor_scalar(gateb[:, ot, :], A[:, :], CR, col(S_SO),
                                           OP.subtract, OP.mult)
                o_v[ot] = inc("sV", ins)

        if debug:
            O_DBG = counts["sV"]
            @blk.sync
            def _(sync):
                sync.wait_ge(sems["sV"], O_DBG)
                for ot in range(NO):
                    sync.dma_start(out=dbg_gate[ot, :, :],
                                   in_=gateb[:, ot, :]).then_inc(sems["sO"], 16)
                    counts["sO"] += 16

        barrier()
        c_s.close()

        d_s = contextlib.ExitStack()
        wdh = d_s.enter_context(nc.sbuf_tensor("wdh", [128, NO, 512], BF16, side="right"))
        ln2log = d_s.enter_context(nc.sbuf_tensor("ln2log", [128, 3 * NT], F32, side="right")) if debug else None
        sqscr2 = d_s.enter_context(nc.sbuf_tensor("sqscr2", [128, D + 128], F32, side="right"))

        dn_recs = {}
        DN_PE_H0 = 0
        for h in range(2):
            @blk.sync
            def _(sync, h=h):
                if h == 0:
                    sync.wait_ge(sems["sC"], AGD_DONE)
                else:
                    sync.wait_ge(sems["sP"], DN_PE_H0)
                sync.dma_start(
                    out=wdh[:, :, :],
                    in_=agd_out.ap()[:, 512 * h:512 * (h + 1)].rearrange(
                        "(ot p) o -> p ot o", p=128)).then_inc(sems["sL"], 16)
                counts["sL"] += 16
            WDH = counts["sL"]

            def dn_group(it):
                def emit(tensor, psum):
                    last = None
                    for ot in range(NO):
                        last = tensor.matmul(psum[:, :],
                                             gateb[:, ot, 128 * it:128 * (it + 1)],
                                             wdh[:, ot, :],
                                             start=(ot == 0), stop=(ot == NO - 1))
                    return last
                return emit

            def mk_dn_evac(h):
                def evac(eng, g, psum):
                    sl = x1[:, g, 512 * h:512 * (h + 1)]
                    return eng.tensor_add(sl, sl, psum[:, :])
                return evac

            recs = mm_groups([dn_group(it) for it in range(NT)], ps[0:3], "vector",
                             mk_dn_evac(h), pre_wait=[("sL", WDH)])
            for g in range(NT):
                dn_recs[(h, g)] = recs[g][1]
            if h == 0:
                DN_PE_H0 = counts["sP"]

        if debug:
            Y20 = dn_recs[(1, 0)]
            @blk.sync
            def _(sync):
                sync.wait_ge(sems["sV"], Y20)
                sync.dma_start(out=dbg_y2[:, :], in_=x1[:, 0, :]).then_inc(sems["sO"], 16)
                sync.dma_start(out=dbg_wdh[:, :, :], in_=wdh[:, :, :]).then_inc(sems["sO"], 16)
                counts["sO"] += 32

        MS2 = sqscr2[:, 1024:1024 + NT]
        SQ2c = sqscr2[:, 1024 + NT:1024 + 2 * NT]
        M2c = sqscr2[:, 1024 + 2 * NT:1024 + 3 * NT]
        V2c = sqscr2[:, 1024 + 3 * NT:1024 + 4 * NT]
        SD2c = sqscr2[:, 1024 + 4 * NT:1024 + 5 * NT]
        I2c = sqscr2[:, 1024 + 5 * NT:1024 + 6 * NT]

        for it in range(NT):
            y2 = dn_recs[(1, it)]

            @blk.vector
            def _(vector, it=it, y2=y2):
                vector.wait_ge(sems["sV"], y2)
                vector.reduce_sum(col(S_MS), x1[:, it, :], AX.X)
                vector.tensor_mul(sqscr2[:, 0:D], x1[:, it, :], x1[:, it, :])
                vector.tensor_copy(MS2[:, it:it + 1], col(S_MS))
                ins = vector.reduce_sum(SQ2c[:, it:it + 1], sqscr2[:, 0:D], AX.X)
                inc("sV", ins)

        @blk.vector
        def _(vector):
            vsep8(vector)
            vector.tensor_scalar_mul(M2c, MS2, 1.0 / D)
            vector.tensor_scalar_mul(V2c, SQ2c, 1.0 / D)
            vsep(vector)
            vector.tensor_mul(SD2c, M2c, M2c)
            vsep(vector)
            vector.tensor_sub(V2c, V2c, SD2c)
            inc("sV", vsettle(vector))
        VARB2 = counts["sV"]

        @blk.scalar
        def _(scalar):
            scalar.wait_ge(sems["sV"], VARB2)
            scalar.activation(SD2c, V2c, ACT.Sqrt, bias=stat[:, 26:27], scale=1.0)
            inc("sS", ssettle(scalar))
        STDB2 = counts["sS"]

        @blk.vector
        def _(vector):
            vector.wait_ge(sems["sS"], STDB2)
            vrecip(vector, I2c, SD2c)
            vsep8(vector)
            ins = None
            for it in range(NT):
                ins = vector.tensor_scalar(x1[:, it, :], x1[:, it, :],
                                           M2c[:, it:it + 1], I2c[:, it:it + 1],
                                           OP.subtract, OP.mult)
            inc("sV", ins)
        OUT_RDY = counts["sV"]

        @blk.sync
        def _(sync):
            sync.wait_ge(sems["sV"], OUT_RDY)
            for it in range(NT):
                sync.dma_start(out=y_d[128 * it:128 * (it + 1), :],
                               in_=x1[:, it, :]).then_inc(sems["sO"], 16)
                counts["sO"] += 16

        if debug:
            @blk.sync
            def _(sync):
                sync.dma_start(out=dbg_stat[:, 0:28], in_=stat[:, :]).then_inc(sems["sO"], 16)
                counts["sO"] += 16

        FIN = counts["sO"]

        @blk.sync
        def _(sync):
            sync.wait_ge(sems["sO"], FIN)

        d_s.close()
        ffn.close()
    return nc


def make_in_maps(x, wq, wk, wv, bq, bk, bv, wg, wu, wd):
    import ml_dtypes
    biases = np.zeros((65, D), dtype=np.float32)
    biases[0] = bq
    biases[32] = bk
    biases[64] = bv
    mask = np.where(np.arange(128)[None, :] >= np.arange(128)[:, None],
                    np.float32(0.0), np.float32(MASK_NEG)).astype(ml_dtypes.bfloat16)
    ident = np.eye(128, dtype=ml_dtypes.bfloat16)
    ones512 = np.zeros((65, 512), dtype=np.float32)
    ones512[0] = 1.0
    ones512[32] = 1.0
    ones512[64] = 1.0
    epscol = np.zeros((128, 2), dtype=np.float32)
    epscol[:, 0] = 1e-5
    epscol[:, 1] = CR
    in_maps = []
    for c in range(NC):
        in_maps.append({
            "x": np.ascontiguousarray(x[c]),
            "xT": np.ascontiguousarray(x[c].T),
            "wq": wq, "wk": wk, "wv": wv,
            "biases": biases,
            "wup_s": np.ascontiguousarray(wu[:, 128 * c:128 * (c + 1)]),
            "wgate_s": np.ascontiguousarray(wg[:, 128 * c:128 * (c + 1)]),
            "wdown_s": np.ascontiguousarray(wd[:, 384 * c:384 * (c + 1)]),
            "mask": mask, "ident": ident, "ones512": ones512, "epscol": epscol,
        })
    return in_maps


def kernel(L0_attn_minibatch, L0_attn_Wq, L0_attn_bq, L0_attn_Wk, L0_attn_bk,
           L0_attn_Wv, L0_attn_bv, L0_ffn_w_gate, L0_ffn_w_up, L0_ffn_w_down):
    x = np.asarray(L0_attn_minibatch, np.float32)
    wq = np.ascontiguousarray(np.asarray(L0_attn_Wq, np.float32))
    wk = np.ascontiguousarray(np.asarray(L0_attn_Wk, np.float32))
    wv = np.ascontiguousarray(np.asarray(L0_attn_Wv, np.float32))
    bq = np.asarray(L0_attn_bq, np.float32)
    bk = np.asarray(L0_attn_bk, np.float32)
    bv = np.asarray(L0_attn_bv, np.float32)
    wg = np.asarray(L0_ffn_w_gate, np.float32)
    wu = np.asarray(L0_ffn_w_up, np.float32)
    wd = np.asarray(L0_ffn_w_down, np.float32)

    nc = build()
    in_maps = make_in_maps(x, wq, wk, wv, bq, bk, bv, wg, wu, wd)
    res = run_bass_kernel_spmd(nc, in_maps, core_ids=list(range(NC)))
    return np.stack([np.asarray(res.results[c]["y"], np.float32) for c in range(NC)], 0)


# version 12
# speedup vs baseline: 1.0346x; 1.0346x over previous
import contextlib
import numpy as np

import concourse.bass as bass
import concourse.mybir as mybir
from concourse.bass_utils import run_bass_kernel_spmd

F32 = mybir.dt.float32
BF16 = mybir.dt.bfloat16
AX = mybir.AxisListType
OP = mybir.AluOpType
ACT = mybir.ActivationFunctionType

T = 2048
D = 1024
O = 3072
NT = T // 128
ND = D // 128
NO = O // 128
NC = 8
RG = [list(range(NC))]

CR = float(3 << 22)
INV_QMAX = float(np.float32(1.0) / np.float32(32767.0))
WQMAX_INV = float(np.float32(1.0) / np.float32(7.0))
MASK_NEG = -1.0e5


def build(debug=False, probe=False):
    nc = bass.Bass("TRN2", target_bir_lowering=False)

    x_d = nc.dram_tensor("x", [T, D], F32, kind="ExternalInput").ap()
    xT_d = nc.dram_tensor("xT", [D, T], F32, kind="ExternalInput").ap()
    wq_d = nc.dram_tensor("wq", [D, D], F32, kind="ExternalInput").ap()
    wk_d = nc.dram_tensor("wk", [D, D], F32, kind="ExternalInput").ap()
    wv_d = nc.dram_tensor("wv", [D, D], F32, kind="ExternalInput").ap()
    biases_d = nc.dram_tensor("biases", [65, D], F32, kind="ExternalInput").ap()
    wup_d = nc.dram_tensor("wup_s", [O, 128], F32, kind="ExternalInput").ap()
    wgate_d = nc.dram_tensor("wgate_s", [O, 128], F32, kind="ExternalInput").ap()
    wdown_d = nc.dram_tensor("wdown_s", [D, O // NC], F32, kind="ExternalInput").ap()
    mask_d = nc.dram_tensor("mask", [128, 128], BF16, kind="ExternalInput").ap()
    ident_d = nc.dram_tensor("ident", [128, 128], BF16, kind="ExternalInput").ap()
    ones_d = nc.dram_tensor("ones512", [65, 512], F32, kind="ExternalInput").ap()
    eps_d = nc.dram_tensor("epscol", [128, 2], F32, kind="ExternalInput").ap()
    y_d = nc.dram_tensor("y", [T, D], F32, kind="ExternalOutput").ap()
    if probe:
        dbg_px = nc.dram_tensor("dbg_px", [T, D], F32, kind="ExternalOutput").ap()
        dbg_pp = nc.dram_tensor("dbg_pp", [128, 17408], BF16, kind="ExternalOutput").ap()
        dbg_pv = nc.dram_tensor("dbg_pv", [128, NT, D], BF16, kind="ExternalOutput").ap()
        dbg_pr = nc.dram_tensor("dbg_pr", [128, NT], F32, kind="ExternalOutput").ap()
        dbg_py = nc.dram_tensor("dbg_py", [T, D], F32, kind="ExternalOutput").ap()
        dbg_st0 = nc.dram_tensor("dbg_st0", [128, 28], F32, kind="ExternalOutput").ap()
        dbg_xt0 = nc.dram_tensor("dbg_xt0", [128, D], F32, kind="ExternalOutput").ap()
        dbg_st1 = nc.dram_tensor("dbg_st1", [128, 28], F32, kind="ExternalOutput").ap()
        dbg_arr = nc.dram_tensor("dbg_arr", [128, 96], F32, kind="ExternalOutput").ap()
    if debug:
        dbg_x1 = nc.dram_tensor("dbg_x1", [T, D], F32, kind="ExternalOutput").ap()
        dbg_gate = nc.dram_tensor("dbg_gate", [NO, 128, T], BF16, kind="ExternalOutput").ap()
        dbg_stat = nc.dram_tensor("dbg_stat", [128, 28], F32, kind="ExternalOutput").ap()
        dbg_q = nc.dram_tensor("dbg_q", [128, ND, T], BF16, kind="ExternalOutput").ap()
        dbg_k = nc.dram_tensor("dbg_k", [128, ND, T], BF16, kind="ExternalOutput").ap()
        dbg_v = nc.dram_tensor("dbg_v", [128, NT, D], BF16, kind="ExternalOutput").ap()
        dbg_p = nc.dram_tensor("dbg_p", [128, 17408], BF16, kind="ExternalOutput").ap()
        dbg_rcol = nc.dram_tensor("dbg_rcol", [128, NT], F32, kind="ExternalOutput").ap()
        dbg_xt = nc.dram_tensor("dbg_xt", [128, ND, T], BF16, kind="ExternalOutput").ap()
        dbg_l = nc.dram_tensor("dbg_l", [128, T], F32, kind="ExternalOutput").ap()
        dbg_ln = nc.dram_tensor("dbg_ln", [128, 3 * NT], F32, kind="ExternalOutput").ap()
        dbg_y2 = nc.dram_tensor("dbg_y2", [128, D], F32, kind="ExternalOutput").ap()
        dbg_ln2 = nc.dram_tensor("dbg_ln2", [128, 3 * NT], F32, kind="ExternalOutput").ap()
        dbg_wdh = nc.dram_tensor("dbg_wdh", [128, NO, 512], BF16, kind="ExternalOutput").ap()
        dbg_nm = nc.dram_tensor("dbg_nm", [128, 32], F32, kind="ExternalOutput").ap()

    agw_in = nc.dram_tensor("agw_in", [128, 2 * O], BF16)
    agw_out = nc.dram_tensor("agw_out", [D, 2 * O], BF16)
    agd_in = nc.dram_tensor("agd_in", [O // NC, D], BF16)
    agd_out = nc.dram_tensor("agd_out", [O, D], BF16)
    cc_in = [nc.dram_tensor(f"cc{i}_in", [1, 256], F32) for i in range(4)]
    cc_out = [nc.dram_tensor(f"cc{i}_out", [1, 256], F32) for i in range(4)]
    up_hbm = nc.dram_tensor("up_hbm", [NO, 128, T], BF16)

    ctx = contextlib.ExitStack()
    with ctx:
        sems = {}
        counts = {}
        for name in ("sL", "sG", "sP", "sS", "sV", "sC", "sO", "sA", "sB", "sK", "sGM"):
            sems[name] = ctx.enter_context(nc.semaphore(name))
            counts[name] = 0

        def inc(name, instr, k=1):
            instr.then_inc(sems[name], k)
            counts[name] += k
            return counts[name]

        def wv(eng, name, val):
            if val > 0:
                eng.wait_ge(sems[name], val)

        blk = ctx.enter_context(nc.Block())

        def barrier():
            snap = {k: v for k, v in counts.items() if k not in ("sC", "sO")}
            for engname in ("sync", "tensor", "scalar", "vector", "gpsimd"):
                @getattr(blk, engname)
                def _(eng):
                    for nm, v in snap.items():
                        if v > 0:
                            eng.wait_ge(sems[nm], v)

        maskt = ctx.enter_context(nc.sbuf_tensor("maskt", [128, 128], BF16, side="left"))
        identt = ctx.enter_context(nc.sbuf_tensor("identt", [128, 128], BF16, side="left"))
        onest = ctx.enter_context(nc.sbuf_tensor("onest", [65, 512], F32, side="left"))
        biast = ctx.enter_context(nc.sbuf_tensor("biast", [65, D], F32, side="left"))
        stat = ctx.enter_context(nc.sbuf_tensor("stat", [128, 28], F32, side="left"))
        rowbuf = ctx.enter_context(nc.sbuf_tensor("rowbuf", [1, 260], F32, side="left"))
        rcol = ctx.enter_context(nc.sbuf_tensor("rcol", [128, NT], F32, side="left"))


        (S_MS, S_SQ, S_M, S_VAR, S_INV, S_NM, S_SSUM, S_T0, S_T1, S_RJ,
         S_GMX, S_SX, S_RSX, S_GU, S_GG, S_SU, S_RSU, S_SG, S_RSG,
         S_G2M, S_SG2, S_RSG2, S_OM, S_SO, S_RSO) = range(25)

        S_JNK = 25

        def col(i):
            return stat[:, i:i + 1]

        def ssettle(scalar, func=ACT.Sqrt):
            scalar.activation(stat[:, S_JNK:S_JNK + 1], stat[:, 26:27], func,
                              bias=stat[:, 26:27], scale=1.0)
            return scalar.activation(stat[:, S_JNK:S_JNK + 1], stat[:, 26:27], func,
                                     bias=stat[:, 26:27], scale=1.0)

        def vsettle(vector):
            vector.tensor_copy(stat[:, S_JNK:S_JNK + 1], stat[:, 26:27])
            return vector.tensor_copy(stat[:, S_JNK:S_JNK + 1], stat[:, 26:27])

        def vrecip(vector, out, in_):
            vector.tensor_copy(stat[:, S_JNK:S_JNK + 1], stat[:, 26:27])
            vector.reciprocal(out, in_)
            vector.tensor_copy(stat[:, S_JNK:S_JNK + 1], stat[:, 26:27])
            vector.tensor_copy(stat[:, S_JNK:S_JNK + 1], stat[:, 26:27])
            vector.reciprocal(out, in_)
            vector.tensor_copy(stat[:, S_JNK:S_JNK + 1], stat[:, 26:27])
            return vector.tensor_copy(stat[:, S_JNK:S_JNK + 1], stat[:, 26:27])

        def vsep8(vector):
            for _ in range(8):
                vector.tensor_copy(stat[:, S_JNK:S_JNK + 1], stat[:, 26:27])

        def vsep(vector):
            vector.tensor_copy(stat[:, S_JNK:S_JNK + 1], stat[:, 26:27])
            vector.tensor_copy(stat[:, S_JNK:S_JNK + 1], stat[:, 26:27])

        ps = [ctx.enter_context(nc.psum_tensor(f"ps{i}", [128, 512], F32))
              for i in range(6)]
        psT = [ctx.enter_context(nc.psum_tensor(f"psT{i}", [128, 128], BF16))
               for i in range(2)]

        ps_busy = {}

        def mm_groups(groups, ps_list, evac_engine, evac_fn, pre_wait=None):
            sem_e = "sS" if evac_engine == "scalar" else "sV"
            recs = []
            for g, emit in enumerate(groups):
                pbuf = ps_list[g % len(ps_list)]

                @blk.tensor
                def _(tensor, g=g, emit=emit, pbuf=pbuf):
                    if pre_wait is not None and g == 0:
                        for nm, v in pre_wait:
                            wv(tensor, nm, v)
                    busy = ps_busy.get(id(pbuf))
                    if busy is not None:
                        wv(tensor, busy[0], busy[1])
                    last = emit(tensor, pbuf)
                    inc("sP", last)
                pe_val = counts["sP"]

                @getattr(blk, evac_engine)
                def _(eng, g=g, pbuf=pbuf, pe_val=pe_val):
                    eng.wait_ge(sems["sP"], pe_val)
                    ins = evac_fn(eng, g, pbuf)
                    recs.append((sem_e, inc(sem_e, ins)))
                ps_busy[id(pbuf)] = recs[-1]
            return recs

        def mm_ktile(nk, n_m, emit_one, emit_bias, ps_list, evac_engine,
                     evac_fn, pre_wait=None):
            sem_e = "sS" if evac_engine == "scalar" else "sV"
            pe_vals = []

            @blk.tensor
            def _(tensor):
                if pre_wait:
                    for nm, v in pre_wait:
                        wv(tensor, nm, v)
                for m in range(n_m):
                    busy = ps_busy.get(id(ps_list[m]))
                    if busy is not None:
                        wv(tensor, busy[0], busy[1])
                last_k = nk - 1
                for k in range(nk):
                    for m in range(n_m):
                        ins = emit_one(tensor, k, m, ps_list[m], k == 0,
                                       (k == last_k and emit_bias is None))
                        if k == last_k and emit_bias is None:
                            pe_vals.append(inc("sP", ins))
                if emit_bias is not None:
                    for m in range(n_m):
                        ins = emit_bias(tensor, m, ps_list[m])
                        pe_vals.append(inc("sP", ins))

            recs = []
            for m in range(n_m):
                @getattr(blk, evac_engine)
                def _(eng, m=m):
                    eng.wait_ge(sems["sP"], pe_vals[m])
                    ins = evac_fn(eng, m, ps_list[m])
                    recs.append((sem_e, inc(sem_e, ins)))
                ps_busy[id(ps_list[m])] = recs[-1]
            return recs

        psT_busy = {0: 0, 1: 0}

        def transpose_seq(n, src_fn, dst_fn, first_wait=None):
            tr_vals = []
            ev_vals = []
            for i in range(n):
                @blk.tensor
                def _(tensor, i=i):
                    if i == 0 and first_wait:
                        for nm, v in first_wait:
                            wv(tensor, nm, v)
                    wv(tensor, "sV", psT_busy[i % 2])
                    ins = tensor.transpose(psT[i % 2][:, :], src_fn(i), identt[:, :])
                    tr_vals.append(inc("sP", ins))

                @blk.vector
                def _(vector, i=i):
                    vector.wait_ge(sems["sP"], tr_vals[i])
                    ins = vector.tensor_copy(dst_fn(i), psT[i % 2][:, :])
                    ev_vals.append(inc("sV", ins))
                psT_busy[i % 2] = ev_vals[-1]
            return ev_vals[-1]

        stage_busy = {}

        def load_cast(n, src_fn, dst_fn, stage, sync_first_wait=None):
            ld_vals = []
            cast_base = counts["sV"]
            keys = [stage[0].name, stage[1].name]

            @blk.sync
            def _(sync):
                if sync_first_wait:
                    for nm, v in sync_first_wait:
                        wv(sync, nm, v)
                for i in range(n):
                    if i >= 2:
                        sync.wait_ge(sems["sV"], cast_base + i - 1)
                    else:
                        wv(sync, "sV", stage_busy.get(keys[i % 2], 0))
                    pn = "sA" if i % 2 == 0 else "sB"
                    sync.dma_start(out=stage[i % 2], in_=src_fn(i)).then_inc(sems[pn], 16)
                    counts[pn] += 16
                    ld_vals.append((pn, counts[pn]))

            @blk.vector
            def _(vector):
                for i in range(n):
                    vector.wait_ge(sems[ld_vals[i][0]], ld_vals[i][1])
                    ins = vector.tensor_copy(dst_fn(i), stage[i % 2])
                    inc("sV", ins)
            for i in range(n):
                stage_busy[keys[i % 2]] = cast_base + i + 1
            return counts["sV"]

        gm_last_ind = [0]

        def global_max(idx, src_cols, dst_cols, wait_pre):
            k = len(src_cols)

            @blk.sync
            def _(sync):
                for nm, v in wait_pre:
                    wv(sync, nm, v)
                wv(sync, "sL", gm_last_ind[0])
                for j, sc in enumerate(src_cols):
                    sync.dma_start(out=rowbuf[0:1, 128 * j:128 * (j + 1)],
                                   in_=sc).then_inc(sems["sL"], 16)
                    counts["sL"] += 16
            ROWS = counts["sL"]

            @blk.vector
            def _(vector):
                vector.wait_ge(sems["sL"], ROWS)
                for j in range(k):
                    vector.reduce_max(rowbuf[0:1, 256 + j:257 + j],
                                      rowbuf[0:1, 128 * j:128 * (j + 1)], AX.X)
                vsep(vector)
                for j in range(k):
                    vector.tensor_scalar_mul(rowbuf[0:1, 128 * j:128 * (j + 1)],
                                             onest[0:1, 0:128],
                                             rowbuf[0:1, 256 + j:257 + j])
                inc("sV", vsettle(vector))
            REP = counts["sV"]

            @blk.sync
            def _(sync):
                sync.wait_ge(sems["sV"], REP)
                sync.dma_start(out=cc_in[idx].ap()[:, 0:128 * k],
                               in_=rowbuf[0:1, 0:128 * k]).then_inc(sems["sL"], 16)
                counts["sL"] += 16
            gm_last_ind[0] = counts["sL"]
            IND = counts["sL"]

            @blk.gpsimd
            def _(gpsimd):
                gpsimd.wait_ge(sems["sL"], IND)
                gpsimd.collective_compute(
                    "AllReduce", OP.max, replica_groups=RG,
                    ins=[cc_in[idx].ap()[:, 0:128 * k].opt()],
                    outs=[cc_out[idx].ap()[:, 0:128 * k].opt()],
                ).then_inc(sems["sC"], 1)
                counts["sC"] += 1
                gpsimd.wait_ge(sems["sC"], counts["sC"])
                for j, dc in enumerate(dst_cols):
                    gpsimd.dma_start(
                        out=dc,
                        in_=cc_out[idx].ap()[:, 128 * j:128 * (j + 1)].rearrange(
                            "a b -> b a")).then_inc(sems["sGM"], 16)
                    counts["sGM"] += 16
            return counts["sGM"]

        atn = contextlib.ExitStack()
        Vb = atn.enter_context(nc.sbuf_tensor("Vb", [128, NT, D], BF16, side="right"))
        POFF = []
        off = 0
        for jt in range(NT):
            POFF.append(off)
            off += T - 128 * jt
        PSZ = off
        Pb = atn.enter_context(nc.sbuf_tensor("Pb", [128, PSZ], BF16, side="right"))
        Lst = atn.enter_context(nc.sbuf_tensor("Lst", [128, T], F32, side="right"))
        nmlog = atn.enter_context(nc.sbuf_tensor("nmlog", [128, 2 * NT], F32, side="right")) if debug else None
        lnlog = atn.enter_context(nc.sbuf_tensor("lnlog", [128, 3 * NT], F32, side="right")) if debug else None
        qk_s = contextlib.ExitStack()
        QTb = qk_s.enter_context(nc.sbuf_tensor("QTb", [128, ND, T], BF16, side="right"))
        KTb = qk_s.enter_context(nc.sbuf_tensor("KTb", [128, ND, T], BF16, side="right"))

        dq_s = contextlib.ExitStack()
        wsl = dq_s.enter_context(nc.sbuf_tensor("wsl", [128, O], F32, side="right"))
        wqs = dq_s.enter_context(nc.sbuf_tensor("wqs", [128, O], BF16, side="right"))
        wtr = dq_s.enter_context(nc.sbuf_tensor("wtr", [128, O], BF16, side="right"))
        dstat = dq_s.enter_context(nc.sbuf_tensor("dstat", [128, 192], F32, side="right"))

        @blk.sync
        def _(sync):
            sync.dma_start(out=maskt[:, :], in_=mask_d[:, :]).then_inc(sems["sK"], 16)
            sync.dma_start(out=identt[:, :], in_=ident_d[:, :]).then_inc(sems["sK"], 16)
            sync.dma_start(out=onest[:, :], in_=ones_d[:, :]).then_inc(sems["sK"], 16)
            sync.dma_start(out=biast[:, :], in_=biases_d[:, :]).then_inc(sems["sK"], 16)
            sync.dma_start(out=stat[:, 26:28], in_=eps_d[:, :]).then_inc(sems["sK"], 16)
            counts["sK"] += 80
            sync.dma_start(out=wsl[:].rearrange("p (a b) -> p a b", b=128),
                           in_=wup_d.rearrange("(a p) b -> p a b", p=128)).then_inc(sems["sL"], 16)
            counts["sL"] += 16
        LD_WUP = counts["sL"]

        def emit_dequant(ld_val, pre=None):
            @blk.vector
            def _(vector):
                if pre is not None:
                    wv(vector, pre[0], pre[1])
                vector.wait_ge(sems["sL"], ld_val)
                w3 = wsl[:].rearrange("p (a k) -> p a k", k=32)
                bmax = dstat[:, 0:96]
                s_ = dstat[:, 96:192]
                vector.tensor_reduce(bmax, w3, AX.X, OP.max, apply_absolute_value=True)
                vsep(vector)
                vector.tensor_scalar(s_, bmax, WQMAX_INV, 1e-12, OP.mult, OP.max)
                vsep(vector)
                vrecip(vector, bmax, s_)
                vsep(vector)
                rs3 = bmax.unsqueeze(2).broadcast_to([128, 96, 32])
                s3 = s_.unsqueeze(2).broadcast_to([128, 96, 32])
                vector.tensor_tensor(w3, w3, rs3, OP.mult)
                vector.tensor_scalar(w3, w3, CR, CR, OP.add, OP.subtract)
                o3 = wqs[:].rearrange("p (a k) -> p a k", k=32)
                ins = vector.tensor_tensor(o3, w3, s3, OP.mult)
                inc("sV", ins)
            return counts["sV"]

        wqs24 = wqs[:].rearrange("p (a b) -> p a b", b=128)
        DQ_UP = emit_dequant(LD_WUP)
        TR_UP = transpose_seq(
            NO, lambda i: wqs24[:, i, :],
            lambda i: wtr[:, 128 * i:128 * (i + 1)],
            first_wait=[("sV", DQ_UP), ("sK", 80)])

        @blk.gpsimd
        def _(gpsimd):
            gpsimd.wait_ge(sems["sV"], TR_UP)
            gpsimd.dma_start(out=agw_in.ap()[:, 0:O], in_=wtr[:, :]).then_inc(sems["sG"], 16)
            counts["sG"] += 16
        WUP_D = counts["sG"]

        @blk.sync
        def _(sync):
            sync.wait_ge(sems["sV"], DQ_UP)
            sync.dma_start(out=wsl[:].rearrange("p (a b) -> p a b", b=128),
                           in_=wgate_d.rearrange("(a p) b -> p a b", p=128)).then_inc(sems["sL"], 16)
            counts["sL"] += 16
        LD_WGATE = counts["sL"]

        DQ_GATE = emit_dequant(LD_WGATE, pre=("sG", WUP_D))
        TR_GATE = transpose_seq(
            NO, lambda i: wqs24[:, i, :],
            lambda i: wtr[:, 128 * i:128 * (i + 1)],
            first_wait=[("sV", DQ_GATE), ("sG", WUP_D)])

        @blk.gpsimd
        def _(gpsimd):
            gpsimd.wait_ge(sems["sV"], TR_GATE)
            gpsimd.dma_start(out=agw_in.ap()[:, O:2 * O], in_=wtr[:, :]).then_inc(sems["sG"], 16)
            counts["sG"] += 16
            gpsimd.wait_ge(sems["sG"], counts["sG"])
            gpsimd.collective_compute(
                "AllGather", OP.bypass, replica_groups=RG,
                ins=[agw_in.ap().opt()], outs=[agw_out.ap().opt()],
            ).then_inc(sems["sC"], 1)
            counts["sC"] += 1
        WGATE_D = counts["sG"]
        AGW_DONE = counts["sC"]

        @blk.sync
        def _(sync):
            sync.wait_ge(sems["sV"], DQ_GATE)
            sync.dma_start(out=wsl[:].rearrange("p (a b) -> p a b", b=384),
                           in_=wdown_d.rearrange("(a p) b -> p a b", p=128)).then_inc(sems["sL"], 16)
            counts["sL"] += 16
        LD_WDOWN = counts["sL"]

        DQ_DOWN = emit_dequant(LD_WDOWN, pre=("sG", WGATE_D))
        wqs_d3 = wqs[:].rearrange("p (a b) -> p a b", b=384)
        wtr_d3 = wtr[:].rearrange("p (a b) -> p a b", b=1024)

        def _src_dn(i):
            ht, oc = divmod(i, 3)
            return wqs_d3[:, ht, 128 * oc:128 * (oc + 1)]

        def _dst_dn(i):
            ht, oc = divmod(i, 3)
            return wtr_d3[:, oc, 128 * ht:128 * (ht + 1)]

        TR_DOWN = transpose_seq(NO, _src_dn, _dst_dn, first_wait=[("sV", DQ_DOWN)])

        @blk.gpsimd
        def _(gpsimd):
            gpsimd.wait_ge(sems["sV"], TR_DOWN)
            gpsimd.dma_start(out=agd_in.ap().rearrange("(a p) b -> p a b", p=128),
                             in_=wtr_d3).then_inc(sems["sG"], 16)
            counts["sG"] += 16
            gpsimd.wait_ge(sems["sG"], counts["sG"])
            gpsimd.collective_compute(
                "AllGather", OP.bypass, replica_groups=RG,
                ins=[agd_in.ap().opt()], outs=[agd_out.ap().opt()],
            ).then_inc(sems["sC"], 1)
            counts["sC"] += 1
        AGD_DONE = counts["sC"]

        barrier()
        dq_s.close()

        a1 = contextlib.ExitStack()
        xTb = a1.enter_context(nc.sbuf_tensor("xTb", [128, ND, T], BF16, side="right"))
        wstage = [a1.enter_context(nc.sbuf_tensor(f"wst{i}", [128, D], F32, side="right"))
                  for i in range(2)]
        wbf = a1.enter_context(nc.sbuf_tensor("wbf", [128, ND, D], BF16, side="right"))

        XT_CAST = load_cast(
            2 * ND,
            lambda i: xT_d[128 * (i // 2):128 * (i // 2 + 1), 1024 * (i % 2):1024 * (i % 2 + 1)],
            lambda i: xTb[:, i // 2, 1024 * (i % 2):1024 * (i % 2 + 1)],
            [wstage[0][:, :], wstage[1][:, :]])

        WQ_CAST = load_cast(
            ND, lambda i: wq_d[128 * i:128 * (i + 1), :],
            lambda i: wbf[:, i, :], [wstage[0][:, :], wstage[1][:, :]])

        def qk_emit(bias_row, kt):
            def emit(tensor, k, m, psum, start, stop):
                return tensor.matmul(psum[:, :], wbf[:, k, 128 * kt:128 * (kt + 1)],
                                     xTb[:, k, 512 * m:512 * (m + 1)],
                                     start=start, stop=stop, skip_group_check=True)
            def bias(tensor, m, psum):
                return tensor.matmul(
                    psum[:, :],
                    biast[32 * bias_row:32 * bias_row + 1, 128 * kt:128 * (kt + 1)],
                    onest[32 * bias_row:32 * bias_row + 1, :], start=False, stop=True,
                    skip_group_check=True)
            return emit, bias

        if debug:
            @blk.sync
            def _(sync):
                sync.wait_ge(sems["sV"], XT_CAST)
                sync.dma_start(out=dbg_xt[:, :, :], in_=xTb[:, :, :]).then_inc(sems["sO"], 16)
                counts["sO"] += 16

        for kt in range(ND):
            e_, b_ = qk_emit(0, kt)
            mm_ktile(ND, 4, e_, b_, ps[0:4], "scalar",
                     lambda eng, m, psum, kt=kt: eng.copy(
                         QTb[:, kt, 512 * m:512 * (m + 1)], psum[:, :]),
                     pre_wait=([("sV", WQ_CAST), ("sV", XT_CAST)] if kt == 0 else None))
        Q_MM = counts["sP"]

        WK_CAST = load_cast(
            ND, lambda i: wk_d[128 * i:128 * (i + 1), :],
            lambda i: wbf[:, i, :], [wstage[0][:, :], wstage[1][:, :]],
            sync_first_wait=[("sP", Q_MM)])

        for kt in range(ND):
            e_, b_ = qk_emit(1, kt)
            mm_ktile(ND, 4, e_, b_, ps[0:4], "scalar",
                     lambda eng, m, psum, kt=kt: eng.copy(
                         KTb[:, kt, 512 * m:512 * (m + 1)], psum[:, :]),
                     pre_wait=([("sV", WK_CAST)] if kt == 0 else None))
        K_MM = counts["sP"]

        WV_CAST = load_cast(
            ND, lambda i: wv_d[128 * i:128 * (i + 1), :],
            lambda i: wbf[:, i, :], [wstage[0][:, :], wstage[1][:, :]],
            sync_first_wait=[("sP", K_MM)])

        def v_emit(jt):
            def emit(tensor, k, m, psum, start, stop):
                return tensor.matmul(psum[:, :], xTb[:, k, 128 * jt:128 * (jt + 1)],
                                     wbf[:, k, 512 * m:512 * (m + 1)],
                                     start=start, stop=stop, skip_group_check=True)
            def bias(tensor, m, psum):
                return tensor.matmul(
                    psum[:, :], onest[64:65, 0:128],
                    biast[64:65, 512 * m:512 * (m + 1)],
                    start=False, stop=True, skip_group_check=True)
            return emit, bias

        for jt in range(NT):
            e_, b_ = v_emit(jt)
            mm_ktile(ND, 2, e_, b_, ps[0:2], "scalar",
                     lambda eng, m, psum, jt=jt: eng.copy(
                         Vb[:, jt, 512 * m:512 * (m + 1)], psum[:, :]),
                     pre_wait=([("sV", WV_CAST)] if jt == 0 else None))

        barrier()
        a1.close()

        jt_exp = {}
        for jt in range(NT):
            Li = T - 128 * jt
            nch = (Li + 511) // 512

            def l_emit(jt):
                def emit(tensor, k, m, psum, start, stop):
                    i0 = 128 * jt + 512 * m
                    n = min(512, T - i0)
                    return tensor.matmul(
                        psum[:, 0:n], KTb[:, k, 128 * jt:128 * (jt + 1)],
                        QTb[:, k, i0:i0 + n],
                        start=start, stop=stop, skip_group_check=True)
                return emit

            def mk_evac(jt):
                def evac(eng, m, psum):
                    if m == 0 and jt >= 1:
                        eng.wait_ge(sems["sS"], jt_exp[jt - 1])
                    i0r = 512 * m
                    n = min(512, (T - 128 * jt) - i0r)
                    return eng.tensor_copy(Lst[:, i0r:i0r + n], psum[:, 0:n])
                return evac

            mm_ktile(ND, nch, l_emit(jt), None, ps[0:4], "vector", mk_evac(jt))
            LJT = counts["sV"]

            @blk.vector
            def _(vector, jt=jt, Li=Li, LJT=LJT):
                vector.tensor_tensor(Lst[:, 0:128], Lst[:, 0:128], maskt[:, :], OP.add)
                vector.tensor_reduce(col(S_NM), Lst[:, 0:Li], AX.X, OP.max,
                                     negate=True)
                vector.tensor_copy(stat[:, S_JNK:S_JNK + 1], stat[:, 26:27])
                ins = vector.tensor_copy(stat[:, S_JNK:S_JNK + 1], stat[:, 26:27])
                inc("sV", ins)
            NM = counts["sV"]

            @blk.scalar
            def _(scalar, jt=jt, Li=Li, NM=NM):
                scalar.wait_ge(sems["sV"], NM)
                scalar.activation(Pb[:, POFF[jt]:POFF[jt] + Li], Lst[:, 0:Li],
                                  ACT.Exp, bias=col(S_NM), scale=1.0,
                                  accum_out=col(S_SSUM))
                scalar.activation(stat[:, S_JNK:S_JNK + 1], stat[:, 26:27], ACT.Exp)
                ins = scalar.activation(stat[:, S_JNK:S_JNK + 1], stat[:, 26:27], ACT.Exp)
                jt_exp[jt] = inc("sS", ins)

            @blk.vector
            def _(vector, jt=jt):
                vector.wait_ge(sems["sS"], jt_exp[jt])
                if debug:
                    vector.tensor_copy(nmlog[:, jt:jt + 1], col(S_NM))
                    vector.tensor_copy(nmlog[:, NT + jt:NT + jt + 1], col(S_SSUM))
                ins = vector.tensor_copy(rcol[:, jt:jt + 1], col(S_SSUM))
                inc("sV", ins)

        if debug:
            LAST_EXP = jt_exp[NT - 1]
            @blk.sync
            def _(sync):
                sync.wait_ge(sems["sS"], LAST_EXP)
                sync.dma_start(out=dbg_l[:, :], in_=Lst[:, :]).then_inc(sems["sO"], 16)
                sync.dma_start(out=dbg_nm[:, :], in_=nmlog[:, :]).then_inc(sems["sO"], 16)
                counts["sO"] += 32

        if debug:
            A2_DONE_V = counts["sV"]
            @blk.sync
            def _(sync):
                sync.wait_ge(sems["sV"], A2_DONE_V)
                sync.dma_start(out=dbg_q[:, :, :], in_=QTb[:, :, :]).then_inc(sems["sO"], 16)
                sync.dma_start(out=dbg_k[:, :, :], in_=KTb[:, :, :]).then_inc(sems["sO"], 16)
                sync.dma_start(out=dbg_v[:, :, :], in_=Vb[:, :, :]).then_inc(sems["sO"], 16)
                sync.dma_start(out=dbg_p[:, :], in_=Pb[:, :]).then_inc(sems["sO"], 16)
                sync.dma_start(out=dbg_rcol[:, :], in_=rcol[:, :]).then_inc(sems["sO"], 16)
                counts["sO"] += 80

        @blk.vector
        def _(vector):
            t32 = Lst[:, 1024:1024 + NT]
            vsep(vector)
            vector.tensor_scalar_mul(t32, rcol[:, :], 32.0)
            vsep8(vector)
            vrecip(vector, rcol[:, :], t32)
            vsep8(vector)
            ins = None
            for jt in range(NT):
                ins = vector.tensor_scalar_mul(Vb[:, jt, :], Vb[:, jt, :],
                                               rcol[:, jt:jt + 1])
            inc("sV", ins)

        barrier()
        if probe:
            @blk.sync
            def _(sync):
                sync.dma_start(out=dbg_pp[:, :], in_=Pb[:, :]).then_inc(sems["sO"], 16)
                sync.dma_start(out=dbg_pv[:, :, :], in_=Vb[:, :, :]).then_inc(sems["sO"], 16)
                sync.dma_start(out=dbg_pr[:, :], in_=rcol[:, :]).then_inc(sems["sO"], 16)
                counts["sO"] += 48
        qk_s.close()

        x1 = ctx.enter_context(nc.sbuf_tensor("x1", [128, NT, D], F32, side="left"))

        x_ld = []
        @blk.sync
        def _(sync):
            for it in range(NT):
                sync.dma_start(out=x1[:, it, :],
                               in_=x_d[128 * it:128 * (it + 1), :]).then_inc(sems["sL"], 16)
                counts["sL"] += 16
                x_ld.append(counts["sL"])

        def pv_emit(it):
            def emit(tensor, k, m, psum, start, stop):
                lhs = Pb[:, POFF[k] + 128 * (it - k):POFF[k] + 128 * (it - k) + 128]
                return tensor.matmul(psum[:, :], lhs,
                                     Vb[:, k, 512 * m:512 * (m + 1)],
                                     start=start, stop=stop, skip_group_check=True)
            return emit

        def mk_pv_evac(it):
            def evac(eng, m, psum):
                eng.wait_ge(sems["sL"], x_ld[-1])
                sl = x1[:, it, 512 * m:512 * (m + 1)]
                return eng.scalar_tensor_tensor(sl, sl, 2.0, psum[:, :], OP.mult, OP.add)
            return evac

        pv_recs = []
        for it in range(NT):
            r = mm_ktile(it + 1, 2, pv_emit(it), None, ps[0:2], "vector",
                         mk_pv_evac(it))
            pv_recs.extend(r)

        PROBE_Y = [0]
        if probe:
            @blk.sync
            def _(sync):
                sync.wait_ge(sems["sV"], pv_recs[-1][1])
                for it in range(NT):
                    sync.dma_start(out=dbg_py[128 * it:128 * (it + 1), :],
                                   in_=x1[:, it, :]).then_inc(sems["sO"], 16)
                    counts["sO"] += 16
            PROBE_Y[0] = counts["sO"]

        MS16 = Lst[:, 1056:1056 + NT]
        SQ16 = Lst[:, 1056 + NT:1056 + 2 * NT]
        M16 = Lst[:, 1056 + 2 * NT:1056 + 3 * NT]
        V16 = Lst[:, 1056 + 3 * NT:1056 + 4 * NT]
        SD16 = Lst[:, 1056 + 4 * NT:1056 + 5 * NT]
        I16 = Lst[:, 1056 + 5 * NT:1056 + 6 * NT]

        for it in range(NT):
            y_done = pv_recs[2 * it + 1][1]

            @blk.vector
            def _(vector, it=it, y_done=y_done):
                vector.wait_ge(sems["sV"], y_done)
                vector.reduce_sum(col(S_MS), x1[:, it, :], AX.X)
                vector.tensor_mul(Lst[:, 0:D], x1[:, it, :], x1[:, it, :])
                vector.tensor_copy(MS16[:, it:it + 1], col(S_MS))
                ins = vector.reduce_sum(SQ16[:, it:it + 1], Lst[:, 0:D], AX.X)
                inc("sV", ins)

        @blk.vector
        def _(vector):
            vsep8(vector)
            vector.tensor_scalar_mul(M16, MS16, 1.0 / D)
            vector.tensor_scalar_mul(V16, SQ16, 1.0 / D)
            vsep(vector)
            vector.tensor_mul(SD16, M16, M16)
            vsep(vector)
            vector.tensor_sub(V16, V16, SD16)
            inc("sV", vsettle(vector))
        VARB = counts["sV"]

        @blk.scalar
        def _(scalar):
            scalar.wait_ge(sems["sV"], VARB)
            scalar.activation(SD16, V16, ACT.Sqrt, bias=stat[:, 26:27], scale=1.0)
            inc("sS", ssettle(scalar))
        STDB = counts["sS"]

        @blk.vector
        def _(vector):
            vector.wait_ge(sems["sS"], STDB)
            vrecip(vector, I16, SD16)
            vsep8(vector)
            for it in range(NT):
                vector.tensor_scalar(x1[:, it, :], x1[:, it, :],
                                     M16[:, it:it + 1], I16[:, it:it + 1],
                                     OP.subtract, OP.mult)
                vector.tensor_reduce(col(S_T0), x1[:, it, :], AX.X, OP.max,
                                     apply_absolute_value=True)
                vsep(vector)
                if it == 0:
                    vector.tensor_copy(col(S_GMX), col(S_T0))
                else:
                    vector.tensor_max(col(S_GMX), col(S_GMX), col(S_T0))
            inc("sV", vsettle(vector))
        ln1_last = counts["sV"]

        if debug:
            @blk.sync
            def _(sync):
                sync.wait_ge(sems["sV"], ln1_last)
                for it in range(NT):
                    sync.dma_start(out=dbg_x1[128 * it:128 * (it + 1), :],
                                   in_=x1[:, it, :]).then_inc(sems["sO"], 16)
                    counts["sO"] += 16

        barrier()
        if probe:
            @blk.sync
            def _(sync):
                for it in range(NT):
                    sync.dma_start(out=dbg_px[128 * it:128 * (it + 1), :],
                                   in_=x1[:, it, :]).then_inc(sems["sO"], 16)
                    counts["sO"] += 16
                sync.dma_start(out=dbg_arr[:, :], in_=Lst[:, 1056:1152]).then_inc(sems["sO"], 16)
                counts["sO"] += 16
        atn.close()

        ffn = contextlib.ExitStack()
        gateb = ffn.enter_context(nc.sbuf_tensor("gateb", [128, NO, T], BF16, side="right"))
        ffn_xq = contextlib.ExitStack()
        xqT = ffn_xq.enter_context(nc.sbuf_tensor("xqT", [128, ND, T], BF16, side="right"))
        a4_s = contextlib.ExitStack()
        fscr = a4_s.enter_context(nc.sbuf_tensor("fscr", [128, D], F32, side="right"))
        xqst = a4_s.enter_context(nc.sbuf_tensor("xqst", [128, D], BF16, side="right"))

        GM1 = global_max(0, [col(S_GMX)], [col(S_GMX)], [("sV", ln1_last)])

        @blk.vector
        def _(vector):
            vector.wait_ge(sems["sGM"], GM1)
            vector.tensor_scalar(col(S_SX), col(S_GMX), INV_QMAX, 1e-12, OP.mult, OP.max)
            vsep(vector)
            vrecip(vector, col(S_RSX), col(S_SX))
            vector.tensor_copy(stat[:, S_JNK:S_JNK + 1], stat[:, 26:27])
            ins = vector.tensor_copy(stat[:, S_JNK:S_JNK + 1], stat[:, 26:27])
            inc("sV", ins)
        SXD = counts["sV"]

        xq_v = {}
        xq_ev = {-1: 0}
        for it in range(NT):
            @blk.scalar
            def _(scalar, it=it):
                if it == 0:
                    scalar.wait_ge(sems["sV"], SXD)
                else:
                    scalar.wait_ge(sems["sV"], xq_v[it - 1])
                ins = scalar.activation(fscr[:, :], x1[:, it, :], ACT.Identity,
                                        bias=stat[:, 27:28], scale=col(S_RSX))
                inc("sS", ins)
            TQ = counts["sS"]

            @blk.vector
            def _(vector, it=it, TQ=TQ):
                vector.wait_ge(sems["sS"], TQ)
                if it >= 1:
                    vector.wait_ge(sems["sV"], xq_ev[it - 1])
                ins = vector.tensor_scalar(xqst[:, :], fscr[:, :],
                                           CR, col(S_SX), OP.subtract, OP.mult)
                xq_v[it] = inc("sV", ins)

            xq_ev[it] = transpose_seq(
                ND,
                lambda kt: xqst[:, 128 * kt:128 * (kt + 1)],
                lambda kt, it=it: xqT[:, kt, 128 * it:128 * (it + 1)],
                first_wait=[("sV", xq_v[it])])

        barrier()
        a4_s.close()

        b_s = contextlib.ExitStack()
        wtile = [b_s.enter_context(nc.sbuf_tensor(f"wtile{i}", [128, ND, 128], BF16, side="right"))
                 for i in range(2)]
        upst = b_s.enter_context(nc.sbuf_tensor("upst", [128, T], BF16, side="right"))

        border = [(w_i, ot) for ot in range(NO) for w_i in range(2)]
        SP_B_BASE = counts["sP"]
        B_PE = [SP_B_BASE + 4 * (j + 1) for j in range(2 * NO)]
        wt_ld = []

        @blk.sync
        def _(sync):
            sync.wait_ge(sems["sC"], AGW_DONE)
            for j, (w_i, ot) in enumerate(border):
                if j >= 2:
                    sync.wait_ge(sems["sP"], B_PE[j - 2])
                ci = w_i * NO + ot
                pn = "sA" if j % 2 == 0 else "sB"
                sync.dma_start(
                    out=wtile[j % 2][:, :, :],
                    in_=agw_out.ap()[:, 128 * ci:128 * (ci + 1)].rearrange(
                        "(ht p) o -> p ht o", p=128)).then_inc(sems[pn], 16)
                counts[pn] += 16
                wt_ld.append((pn, counts[pn]))

        up_dma = {}
        up_max = {}
        gmax_seed = [False, False]
        for j, (w_i, ot) in enumerate(border):
            def b_emit(j):
                def emit(tensor, k, m, psum, start, stop):
                    return tensor.matmul(psum[:, :], wtile[j % 2][:, k, :],
                                         xqT[:, k, 512 * m:512 * (m + 1)],
                                         start=start, stop=stop,
                                         skip_group_check=True)
                return emit

            if w_i == 0:
                def mk_ev_up(ot):
                    def evac(eng, m, psum):
                        if m == 0 and ot >= 1:
                            eng.wait_ge(sems["sG"], up_dma[ot - 1])
                            eng.wait_ge(sems["sV"], up_max[ot - 1])
                        return eng.copy(upst[:, 512 * m:512 * (m + 1)], psum[:, :])
                    return evac
                recs = mm_ktile(ND, 4, b_emit(j), None, ps[0:4], "scalar",
                                mk_ev_up(ot), pre_wait=[wt_ld[j]])
                EV = recs[-1][1]

                @blk.gpsimd
                def _(gpsimd, ot=ot, EV=EV):
                    gpsimd.wait_ge(sems["sS"], EV)
                    gpsimd.dma_start(out=up_hbm.ap()[ot, :, :],
                                     in_=upst[:, :]).then_inc(sems["sG"], 16)
                    counts["sG"] += 16
                    up_dma[ot] = counts["sG"]

                @blk.vector
                def _(vector, ot=ot, EV=EV):
                    vector.wait_ge(sems["sS"], EV)
                    vector.tensor_reduce(col(S_T0), upst[:, :], AX.X, OP.max,
                                         apply_absolute_value=True)
                    vsep(vector)
                    if not gmax_seed[0]:
                        vector.tensor_copy(col(S_GU), col(S_T0))
                        gmax_seed[0] = True
                    else:
                        vector.tensor_max(col(S_GU), col(S_GU), col(S_T0))
                    up_max[ot] = inc("sV", vsettle(vector))
            else:
                def mk_ev_g(ot):
                    def evac(eng, m, psum):
                        return eng.copy(gateb[:, ot, 512 * m:512 * (m + 1)], psum[:, :])
                    return evac
                recs = mm_ktile(ND, 4, b_emit(j), None, ps[0:4], "scalar",
                                mk_ev_g(ot), pre_wait=[wt_ld[j]])
                EV = recs[-1][1]

                @blk.vector
                def _(vector, ot=ot, EV=EV):
                    vector.wait_ge(sems["sS"], EV)
                    vector.tensor_reduce(col(S_T0), gateb[:, ot, :], AX.X, OP.max,
                                         apply_absolute_value=True)
                    vsep(vector)
                    if not gmax_seed[1]:
                        vector.tensor_copy(col(S_GG), col(S_T0))
                        gmax_seed[1] = True
                    else:
                        vector.tensor_max(col(S_GG), col(S_GG), col(S_T0))
                    inc("sV", vsettle(vector))
        B_V_LAST = counts["sV"]

        barrier()
        b_s.close()
        ffn_xq.close()

        GM2 = global_max(1, [col(S_GU), col(S_GG)], [col(S_GU), col(S_GG)],
                         [("sV", B_V_LAST)])

        @blk.vector
        def _(vector):
            vector.wait_ge(sems["sGM"], GM2)
            vector.tensor_scalar(col(S_SU), col(S_GU), INV_QMAX, 1e-12, OP.mult, OP.max)
            vector.tensor_scalar(col(S_SG), col(S_GG), INV_QMAX, 1e-12, OP.mult, OP.max)
            vsep(vector)
            vrecip(vector, col(S_RSU), col(S_SU))
            vsep(vector)
            vrecip(vector, col(S_RSG), col(S_SG))
            vector.tensor_copy(stat[:, S_JNK:S_JNK + 1], stat[:, 26:27])
            ins = vector.tensor_copy(stat[:, S_JNK:S_JNK + 1], stat[:, 26:27])
            inc("sV", ins)
        SC2 = counts["sV"]

        c_s = contextlib.ExitStack()
        scrA = [c_s.enter_context(nc.sbuf_tensor(f"scrA{i}", [128, T], F32, side="right")) for i in range(2)]
        if debug:
            _scrB0 = c_s.enter_context(nc.sbuf_tensor("scrB0", [128, T], F32, side="right"))
            scrB = [_scrB0, _scrB0]
        else:
            scrB = [c_s.enter_context(nc.sbuf_tensor(f"scrB{i}", [128, T], F32, side="right")) for i in range(2)]
        upld = [c_s.enter_context(nc.sbuf_tensor(f"upld{i}", [128, T], BF16, side="right")) for i in range(2)]

        c1a_v = {}
        for ot in range(NO):
            A = scrA[ot % 2]

            @blk.scalar
            def _(scalar, ot=ot, A=A):
                if ot == 0:
                    scalar.wait_ge(sems["sV"], SC2)
                if ot >= 2:
                    scalar.wait_ge(sems["sV"], c1a_v[ot - 2])
                ins = scalar.activation(A[:, :], gateb[:, ot, :], ACT.Identity,
                                        bias=stat[:, 27:28], scale=col(S_RSG))
                inc("sS", ins)
            T1 = counts["sS"]

            @blk.vector
            def _(vector, ot=ot, A=A, T1=T1):
                vector.wait_ge(sems["sS"], T1)
                ins = vector.tensor_scalar(gateb[:, ot, :], A[:, :], CR, col(S_SG),
                                           OP.subtract, OP.mult)
                c1a_v[ot] = inc("sV", ins)
        C1A_LAST = counts["sV"]

        c1_v = {}
        for ot in range(NO):
            A = scrA[ot % 2]
            Bs = scrB[ot % 2]

            @blk.scalar
            def _(scalar, ot=ot, A=A):
                if ot == 0:
                    scalar.wait_ge(sems["sV"], C1A_LAST)
                if ot >= 2:
                    scalar.wait_ge(sems["sV"], c1_v[ot - 2])
                ins = scalar.activation(A[:, :], gateb[:, ot, :], ACT.Sigmoid)
                inc("sS", ins)
            SGD = counts["sS"]

            @blk.vector
            def _(vector, ot=ot, A=A, Bs=Bs, SGD=SGD):
                vector.wait_ge(sems["sS"], SGD)
                vector.tensor_scalar(Bs[:, :], A[:, :], 65536.0, CR, OP.mult, OP.add)
                vector.tensor_scalar(Bs[:, :], Bs[:, :], CR, 65535.0, OP.subtract, OP.min)
                vector.scalar_tensor_tensor(gateb[:, ot, :], gateb[:, ot, :],
                                            1.0 / 65536.0, Bs[:, :], OP.mult, OP.mult)
                vector.tensor_reduce(col(S_T0), gateb[:, ot, :], AX.X, OP.max,
                                     apply_absolute_value=True)
                vsep(vector)
                if ot == 0:
                    vector.tensor_copy(col(S_G2M), col(S_T0))
                else:
                    vector.tensor_max(col(S_G2M), col(S_G2M), col(S_T0))
                c1_v[ot] = inc("sV", vsettle(vector))
        C1_LAST = counts["sV"]

        GM3 = global_max(2, [col(S_G2M)], [col(S_G2M)], [("sV", C1_LAST)])

        @blk.vector
        def _(vector):
            vector.wait_ge(sems["sGM"], GM3)
            vector.tensor_scalar(col(S_SG2), col(S_G2M), INV_QMAX, 1e-12, OP.mult, OP.max)
            vsep(vector)
            vrecip(vector, col(S_RSG2), col(S_SG2))
            vector.tensor_copy(stat[:, S_JNK:S_JNK + 1], stat[:, 26:27])
            ins = vector.tensor_copy(stat[:, S_JNK:S_JNK + 1], stat[:, 26:27])
            inc("sV", ins)
        SC3 = counts["sV"]

        c2_v = {}
        up_rld = {}
        for ot in range(NO):
            A = scrA[ot % 2]
            Bs = scrB[ot % 2]

            @blk.gpsimd
            def _(gpsimd, ot=ot):
                if ot >= 2:
                    gpsimd.wait_ge(sems["sV"], c2_v[ot - 2])
                pn = "sA" if ot % 2 == 0 else "sB"
                gpsimd.dma_start(out=upld[ot % 2][:, :],
                                 in_=up_hbm.ap()[ot, :, :]).then_inc(sems[pn], 16)
                counts[pn] += 16
                up_rld[ot] = (pn, counts[pn])

            @blk.scalar
            def _(scalar, ot=ot, A=A, Bs=Bs):
                if ot == 0:
                    scalar.wait_ge(sems["sV"], SC3)
                if ot >= 2:
                    scalar.wait_ge(sems["sV"], c2_v[ot - 2])
                scalar.activation(A[:, :], gateb[:, ot, :], ACT.Identity,
                                  bias=stat[:, 27:28], scale=col(S_RSG2))
                scalar.wait_ge(sems[up_rld[ot][0]], up_rld[ot][1])
                ins = scalar.activation(Bs[:, :], upld[ot % 2][:, :], ACT.Identity,
                                        bias=stat[:, 27:28], scale=col(S_RSU))
                inc("sS", ins)
            TS = counts["sS"]

            @blk.vector
            def _(vector, ot=ot, A=A, Bs=Bs, TS=TS):
                vector.wait_ge(sems["sS"], TS)
                vector.tensor_scalar(gateb[:, ot, :], A[:, :], CR, col(S_SG2),
                                     OP.subtract, OP.mult)
                vector.tensor_scalar(upld[ot % 2][:, :], Bs[:, :], CR, col(S_SU),
                                     OP.subtract, OP.mult)
                vector.tensor_mul(gateb[:, ot, :], gateb[:, ot, :], upld[ot % 2][:, :])
                vector.tensor_reduce(col(S_T0), gateb[:, ot, :], AX.X, OP.max,
                                     apply_absolute_value=True)
                vsep(vector)
                if ot == 0:
                    vector.tensor_copy(col(S_OM), col(S_T0))
                else:
                    vector.tensor_max(col(S_OM), col(S_OM), col(S_T0))
                c2_v[ot] = inc("sV", vsettle(vector))
        C2_LAST = counts["sV"]

        GM4 = global_max(3, [col(S_OM)], [col(S_OM)], [("sV", C2_LAST)])

        @blk.vector
        def _(vector):
            vector.wait_ge(sems["sGM"], GM4)
            vector.tensor_scalar(col(S_SO), col(S_OM), INV_QMAX, 1e-12, OP.mult, OP.max)
            vsep(vector)
            vrecip(vector, col(S_RSO), col(S_SO))
            vector.tensor_copy(stat[:, S_JNK:S_JNK + 1], stat[:, 26:27])
            ins = vector.tensor_copy(stat[:, S_JNK:S_JNK + 1], stat[:, 26:27])
            inc("sV", ins)
        SC4 = counts["sV"]

        o_v = {}
        for ot in range(NO):
            A = scrA[ot % 2]

            @blk.scalar
            def _(scalar, ot=ot, A=A):
                if ot == 0:
                    scalar.wait_ge(sems["sV"], SC4)
                if ot >= 2:
                    scalar.wait_ge(sems["sV"], o_v[ot - 2])
                ins = scalar.activation(A[:, :], gateb[:, ot, :], ACT.Identity,
                                        bias=stat[:, 27:28], scale=col(S_RSO))
                inc("sS", ins)
            TO = counts["sS"]

            @blk.vector
            def _(vector, ot=ot, A=A, TO=TO):
                vector.wait_ge(sems["sS"], TO)
                ins = vector.tensor_scalar(gateb[:, ot, :], A[:, :], CR, col(S_SO),
                                           OP.subtract, OP.mult)
                o_v[ot] = inc("sV", ins)

        if debug:
            O_DBG = counts["sV"]
            @blk.sync
            def _(sync):
                sync.wait_ge(sems["sV"], O_DBG)
                for ot in range(NO):
                    sync.dma_start(out=dbg_gate[ot, :, :],
                                   in_=gateb[:, ot, :]).then_inc(sems["sO"], 16)
                    counts["sO"] += 16

        barrier()
        c_s.close()

        d_s = contextlib.ExitStack()
        wdh = d_s.enter_context(nc.sbuf_tensor("wdh", [128, NO, 512], BF16, side="right"))
        ln2log = d_s.enter_context(nc.sbuf_tensor("ln2log", [128, 3 * NT], F32, side="right")) if debug else None
        sqscr2 = d_s.enter_context(nc.sbuf_tensor("sqscr2", [128, D + 128], F32, side="right"))

        dn_recs = {}
        DN_PE_H0 = 0
        for h in range(2):
            @blk.sync
            def _(sync, h=h):
                if h == 0:
                    sync.wait_ge(sems["sC"], AGD_DONE)
                else:
                    sync.wait_ge(sems["sP"], DN_PE_H0)
                sync.dma_start(
                    out=wdh[:, :, :],
                    in_=agd_out.ap()[:, 512 * h:512 * (h + 1)].rearrange(
                        "(ot p) o -> p ot o", p=128)).then_inc(sems["sL"], 16)
                counts["sL"] += 16
            WDH = counts["sL"]

            def dn_group(it):
                def emit(tensor, psum):
                    last = None
                    for ot in range(NO):
                        last = tensor.matmul(psum[:, :],
                                             gateb[:, ot, 128 * it:128 * (it + 1)],
                                             wdh[:, ot, :],
                                             start=(ot == 0), stop=(ot == NO - 1))
                    return last
                return emit

            def mk_dn_evac(h):
                def evac(eng, g, psum):
                    sl = x1[:, g, 512 * h:512 * (h + 1)]
                    return eng.tensor_add(sl, sl, psum[:, :])
                return evac

            recs = mm_groups([dn_group(it) for it in range(NT)], ps[0:3], "vector",
                             mk_dn_evac(h), pre_wait=[("sL", WDH)])
            for g in range(NT):
                dn_recs[(h, g)] = recs[g][1]
            if h == 0:
                DN_PE_H0 = counts["sP"]

        if debug:
            Y20 = dn_recs[(1, 0)]
            @blk.sync
            def _(sync):
                sync.wait_ge(sems["sV"], Y20)
                sync.dma_start(out=dbg_y2[:, :], in_=x1[:, 0, :]).then_inc(sems["sO"], 16)
                sync.dma_start(out=dbg_wdh[:, :, :], in_=wdh[:, :, :]).then_inc(sems["sO"], 16)
                counts["sO"] += 32

        MS2 = sqscr2[:, 1024:1024 + NT]
        SQ2c = sqscr2[:, 1024 + NT:1024 + 2 * NT]
        M2c = sqscr2[:, 1024 + 2 * NT:1024 + 3 * NT]
        V2c = sqscr2[:, 1024 + 3 * NT:1024 + 4 * NT]
        SD2c = sqscr2[:, 1024 + 4 * NT:1024 + 5 * NT]
        I2c = sqscr2[:, 1024 + 5 * NT:1024 + 6 * NT]

        for it in range(NT):
            y2 = dn_recs[(1, it)]

            @blk.vector
            def _(vector, it=it, y2=y2):
                vector.wait_ge(sems["sV"], y2)
                vector.reduce_sum(col(S_MS), x1[:, it, :], AX.X)
                vector.tensor_mul(sqscr2[:, 0:D], x1[:, it, :], x1[:, it, :])
                vector.tensor_copy(MS2[:, it:it + 1], col(S_MS))
                ins = vector.reduce_sum(SQ2c[:, it:it + 1], sqscr2[:, 0:D], AX.X)
                inc("sV", ins)

        @blk.vector
        def _(vector):
            vsep8(vector)
            vector.tensor_scalar_mul(M2c, MS2, 1.0 / D)
            vector.tensor_scalar_mul(V2c, SQ2c, 1.0 / D)
            vsep(vector)
            vector.tensor_mul(SD2c, M2c, M2c)
            vsep(vector)
            vector.tensor_sub(V2c, V2c, SD2c)
            inc("sV", vsettle(vector))
        VARB2 = counts["sV"]

        @blk.scalar
        def _(scalar):
            scalar.wait_ge(sems["sV"], VARB2)
            scalar.activation(SD2c, V2c, ACT.Sqrt, bias=stat[:, 26:27], scale=1.0)
            inc("sS", ssettle(scalar))
        STDB2 = counts["sS"]

        @blk.vector
        def _(vector):
            vector.wait_ge(sems["sS"], STDB2)
            vrecip(vector, I2c, SD2c)
            vsep8(vector)
            ins = None
            for it in range(NT):
                ins = vector.tensor_scalar(x1[:, it, :], x1[:, it, :],
                                           M2c[:, it:it + 1], I2c[:, it:it + 1],
                                           OP.subtract, OP.mult)
            inc("sV", ins)
        OUT_RDY = counts["sV"]

        @blk.sync
        def _(sync):
            sync.wait_ge(sems["sV"], OUT_RDY)
            for it in range(NT):
                sync.dma_start(out=y_d[128 * it:128 * (it + 1), :],
                               in_=x1[:, it, :]).then_inc(sems["sO"], 16)
                counts["sO"] += 16

        if debug:
            @blk.sync
            def _(sync):
                sync.dma_start(out=dbg_stat[:, 0:28], in_=stat[:, :]).then_inc(sems["sO"], 16)
                counts["sO"] += 16

        FIN = counts["sO"]

        @blk.sync
        def _(sync):
            sync.wait_ge(sems["sO"], FIN)

        d_s.close()
        ffn.close()
    return nc


def make_in_maps(x, wq, wk, wv, bq, bk, bv, wg, wu, wd):
    import ml_dtypes
    biases = np.zeros((65, D), dtype=np.float32)
    biases[0] = bq
    biases[32] = bk
    biases[64] = bv
    mask = np.where(np.arange(128)[None, :] >= np.arange(128)[:, None],
                    np.float32(0.0), np.float32(MASK_NEG)).astype(ml_dtypes.bfloat16)
    ident = np.eye(128, dtype=ml_dtypes.bfloat16)
    ones512 = np.zeros((65, 512), dtype=np.float32)
    ones512[0] = 1.0
    ones512[32] = 1.0
    ones512[64] = 1.0
    epscol = np.zeros((128, 2), dtype=np.float32)
    epscol[:, 0] = 1e-5
    epscol[:, 1] = CR
    in_maps = []
    for c in range(NC):
        in_maps.append({
            "x": np.ascontiguousarray(x[c]),
            "xT": np.ascontiguousarray(x[c].T),
            "wq": wq, "wk": wk, "wv": wv,
            "biases": biases,
            "wup_s": np.ascontiguousarray(wu[:, 128 * c:128 * (c + 1)]),
            "wgate_s": np.ascontiguousarray(wg[:, 128 * c:128 * (c + 1)]),
            "wdown_s": np.ascontiguousarray(wd[:, 384 * c:384 * (c + 1)]),
            "mask": mask, "ident": ident, "ones512": ones512, "epscol": epscol,
        })
    return in_maps


def kernel(L0_attn_minibatch, L0_attn_Wq, L0_attn_bq, L0_attn_Wk, L0_attn_bk,
           L0_attn_Wv, L0_attn_bv, L0_ffn_w_gate, L0_ffn_w_up, L0_ffn_w_down):
    x = np.asarray(L0_attn_minibatch, np.float32)
    wq = np.ascontiguousarray(np.asarray(L0_attn_Wq, np.float32))
    wk = np.ascontiguousarray(np.asarray(L0_attn_Wk, np.float32))
    wv = np.ascontiguousarray(np.asarray(L0_attn_Wv, np.float32))
    bq = np.asarray(L0_attn_bq, np.float32)
    bk = np.asarray(L0_attn_bk, np.float32)
    bv = np.asarray(L0_attn_bv, np.float32)
    wg = np.asarray(L0_ffn_w_gate, np.float32)
    wu = np.asarray(L0_ffn_w_up, np.float32)
    wd = np.asarray(L0_ffn_w_down, np.float32)

    nc = build()
    in_maps = make_in_maps(x, wq, wk, wv, bq, bk, bv, wg, wu, wd)
    res = run_bass_kernel_spmd(nc, in_maps, core_ids=list(range(NC)))
    return np.stack([np.asarray(res.results[c]["y"], np.float32) for c in range(NC)], 0)


# version 13
# speedup vs baseline: 1.2049x; 1.1646x over previous
import contextlib
import numpy as np

import concourse.bass as bass
import concourse.mybir as mybir
from concourse.bass_utils import run_bass_kernel_spmd

F32 = mybir.dt.float32
BF16 = mybir.dt.bfloat16
AX = mybir.AxisListType
OP = mybir.AluOpType
ACT = mybir.ActivationFunctionType

T = 2048
D = 1024
O = 3072
NT = T // 128
ND = D // 128
NO = O // 128
NC = 8
RG = [list(range(NC))]

CR = float(3 << 22)
INV_QMAX = float(np.float32(1.0) / np.float32(32767.0))
WQMAX_INV = float(np.float32(1.0) / np.float32(7.0))
MASK_NEG = -1.0e5


def build(debug=False, probe=False):
    nc = bass.Bass("TRN2", target_bir_lowering=False)

    x_d = nc.dram_tensor("x", [T, D], F32, kind="ExternalInput").ap()
    xT_d = nc.dram_tensor("xT", [D, T], F32, kind="ExternalInput").ap()
    wq_d = nc.dram_tensor("wq", [D, D], F32, kind="ExternalInput").ap()
    wk_d = nc.dram_tensor("wk", [D, D], F32, kind="ExternalInput").ap()
    wv_d = nc.dram_tensor("wv", [D, D], F32, kind="ExternalInput").ap()
    biases_d = nc.dram_tensor("biases", [65, D], F32, kind="ExternalInput").ap()
    wup_d = nc.dram_tensor("wup_s", [O, 128], F32, kind="ExternalInput").ap()
    wgate_d = nc.dram_tensor("wgate_s", [O, 128], F32, kind="ExternalInput").ap()
    wdown_d = nc.dram_tensor("wdown_s", [D, O // NC], F32, kind="ExternalInput").ap()
    mask_d = nc.dram_tensor("mask", [128, 128], BF16, kind="ExternalInput").ap()
    ident_d = nc.dram_tensor("ident", [128, 128], BF16, kind="ExternalInput").ap()
    ones_d = nc.dram_tensor("ones512", [65, 512], F32, kind="ExternalInput").ap()
    eps_d = nc.dram_tensor("epscol", [128, 2], F32, kind="ExternalInput").ap()
    y_d = nc.dram_tensor("y", [T, D], F32, kind="ExternalOutput").ap()
    if probe:
        dbg_px = nc.dram_tensor("dbg_px", [T, D], F32, kind="ExternalOutput").ap()
        dbg_pp = nc.dram_tensor("dbg_pp", [128, 17408], BF16, kind="ExternalOutput").ap()
        dbg_pv = nc.dram_tensor("dbg_pv", [128, NT, D], BF16, kind="ExternalOutput").ap()
        dbg_pr = nc.dram_tensor("dbg_pr", [128, NT], F32, kind="ExternalOutput").ap()
        dbg_py = nc.dram_tensor("dbg_py", [T, D], F32, kind="ExternalOutput").ap()
        dbg_st0 = nc.dram_tensor("dbg_st0", [128, 28], F32, kind="ExternalOutput").ap()
        dbg_xt0 = nc.dram_tensor("dbg_xt0", [128, D], F32, kind="ExternalOutput").ap()
        dbg_st1 = nc.dram_tensor("dbg_st1", [128, 28], F32, kind="ExternalOutput").ap()
        dbg_arr = nc.dram_tensor("dbg_arr", [128, 96], F32, kind="ExternalOutput").ap()
    if debug:
        dbg_x1 = nc.dram_tensor("dbg_x1", [T, D], F32, kind="ExternalOutput").ap()
        dbg_gate = nc.dram_tensor("dbg_gate", [NO, 128, T], BF16, kind="ExternalOutput").ap()
        dbg_stat = nc.dram_tensor("dbg_stat", [128, 28], F32, kind="ExternalOutput").ap()
        dbg_q = nc.dram_tensor("dbg_q", [128, ND, T], BF16, kind="ExternalOutput").ap()
        dbg_k = nc.dram_tensor("dbg_k", [128, ND, T], BF16, kind="ExternalOutput").ap()
        dbg_v = nc.dram_tensor("dbg_v", [128, NT, D], BF16, kind="ExternalOutput").ap()
        dbg_p = nc.dram_tensor("dbg_p", [128, 17408], BF16, kind="ExternalOutput").ap()
        dbg_rcol = nc.dram_tensor("dbg_rcol", [128, NT], F32, kind="ExternalOutput").ap()
        dbg_xt = nc.dram_tensor("dbg_xt", [128, ND, T], BF16, kind="ExternalOutput").ap()
        dbg_l = nc.dram_tensor("dbg_l", [128, T], F32, kind="ExternalOutput").ap()
        dbg_ln = nc.dram_tensor("dbg_ln", [128, 3 * NT], F32, kind="ExternalOutput").ap()
        dbg_y2 = nc.dram_tensor("dbg_y2", [128, D], F32, kind="ExternalOutput").ap()
        dbg_ln2 = nc.dram_tensor("dbg_ln2", [128, 3 * NT], F32, kind="ExternalOutput").ap()
        dbg_wdh = nc.dram_tensor("dbg_wdh", [128, NO, 512], BF16, kind="ExternalOutput").ap()
        dbg_nm = nc.dram_tensor("dbg_nm", [128, 32], F32, kind="ExternalOutput").ap()

    agw_in = nc.dram_tensor("agw_in", [128, 2 * O], BF16)
    agw_out = nc.dram_tensor("agw_out", [D, 2 * O], BF16)
    agd_in = nc.dram_tensor("agd_in", [O // NC, D], BF16)
    agd_out = nc.dram_tensor("agd_out", [O, D], BF16)
    cc_in = [nc.dram_tensor(f"cc{i}_in", [1, 256], F32) for i in range(4)]
    cc_out = [nc.dram_tensor(f"cc{i}_out", [1, 256], F32) for i in range(4)]
    up_hbm = nc.dram_tensor("up_hbm", [NO, 128, T], BF16)

    ctx = contextlib.ExitStack()
    with ctx:
        sems = {}
        counts = {}
        for name in ("sL", "sG", "sP", "sS", "sV", "sC", "sO", "sA", "sB", "sK", "sGM"):
            sems[name] = ctx.enter_context(nc.semaphore(name))
            counts[name] = 0

        def inc(name, instr, k=1):
            instr.then_inc(sems[name], k)
            counts[name] += k
            return counts[name]

        def wv(eng, name, val):
            if val > 0:
                eng.wait_ge(sems[name], val)

        blk = ctx.enter_context(nc.Block())

        def barrier():
            snap = {k: v for k, v in counts.items() if k not in ("sC", "sO")}
            for engname in ("sync", "tensor", "scalar", "vector", "gpsimd"):
                @getattr(blk, engname)
                def _(eng):
                    for nm, v in snap.items():
                        if v > 0:
                            eng.wait_ge(sems[nm], v)

        maskt = ctx.enter_context(nc.sbuf_tensor("maskt", [128, 128], BF16, side="left"))
        identt = ctx.enter_context(nc.sbuf_tensor("identt", [128, 128], BF16, side="left"))
        onest = ctx.enter_context(nc.sbuf_tensor("onest", [65, 512], F32, side="left"))
        biast = ctx.enter_context(nc.sbuf_tensor("biast", [65, D], F32, side="left"))
        stat = ctx.enter_context(nc.sbuf_tensor("stat", [128, 28], F32, side="left"))
        rowbuf = ctx.enter_context(nc.sbuf_tensor("rowbuf", [1, 260], F32, side="left"))
        rcol = ctx.enter_context(nc.sbuf_tensor("rcol", [128, NT], F32, side="left"))


        (S_MS, S_SQ, S_M, S_VAR, S_INV, S_NM, S_SSUM, S_T0, S_T1, S_RJ,
         S_GMX, S_SX, S_RSX, S_GU, S_GG, S_SU, S_RSU, S_SG, S_RSG,
         S_G2M, S_SG2, S_RSG2, S_OM, S_SO, S_RSO) = range(25)

        S_JNK = 25

        def col(i):
            return stat[:, i:i + 1]

        def ssettle(scalar, func=ACT.Sqrt):
            scalar.activation(stat[:, S_JNK:S_JNK + 1], stat[:, 26:27], func,
                              bias=stat[:, 26:27], scale=1.0)
            return scalar.activation(stat[:, S_JNK:S_JNK + 1], stat[:, 26:27], func,
                                     bias=stat[:, 26:27], scale=1.0)

        def vsettle(vector):
            vector.tensor_copy(stat[:, S_JNK:S_JNK + 1], stat[:, 26:27])
            return vector.tensor_copy(stat[:, S_JNK:S_JNK + 1], stat[:, 26:27])

        def vrecip(vector, out, in_):
            vector.tensor_copy(stat[:, S_JNK:S_JNK + 1], stat[:, 26:27])
            vector.reciprocal(out, in_)
            vector.tensor_copy(stat[:, S_JNK:S_JNK + 1], stat[:, 26:27])
            vector.tensor_copy(stat[:, S_JNK:S_JNK + 1], stat[:, 26:27])
            vector.reciprocal(out, in_)
            vector.tensor_copy(stat[:, S_JNK:S_JNK + 1], stat[:, 26:27])
            return vector.tensor_copy(stat[:, S_JNK:S_JNK + 1], stat[:, 26:27])

        def vsep8(vector):
            for _ in range(8):
                vector.tensor_copy(stat[:, S_JNK:S_JNK + 1], stat[:, 26:27])

        def vsep(vector):
            vector.tensor_copy(stat[:, S_JNK:S_JNK + 1], stat[:, 26:27])
            vector.tensor_copy(stat[:, S_JNK:S_JNK + 1], stat[:, 26:27])

        ps = [ctx.enter_context(nc.psum_tensor(f"ps{i}", [128, 512], F32))
              for i in range(6)]
        psT = [ctx.enter_context(nc.psum_tensor(f"psT{i}", [128, 128], BF16))
               for i in range(2)]

        ps_busy = {}

        def mm_groups(groups, ps_list, evac_engine, evac_fn, pre_wait=None):
            sem_e = "sS" if evac_engine == "scalar" else "sV"
            recs = []
            for g, emit in enumerate(groups):
                pbuf = ps_list[g % len(ps_list)]

                @blk.tensor
                def _(tensor, g=g, emit=emit, pbuf=pbuf):
                    if pre_wait is not None and g == 0:
                        for nm, v in pre_wait:
                            wv(tensor, nm, v)
                    busy = ps_busy.get(id(pbuf))
                    if busy is not None:
                        wv(tensor, busy[0], busy[1])
                    last = emit(tensor, pbuf)
                    inc("sP", last)
                pe_val = counts["sP"]

                @getattr(blk, evac_engine)
                def _(eng, g=g, pbuf=pbuf, pe_val=pe_val):
                    eng.wait_ge(sems["sP"], pe_val)
                    ins = evac_fn(eng, g, pbuf)
                    recs.append((sem_e, inc(sem_e, ins)))
                ps_busy[id(pbuf)] = recs[-1]
            return recs

        psT_busy = {0: 0, 1: 0}

        def transpose_seq(n, src_fn, dst_fn, first_wait=None):
            tr_vals = []
            ev_vals = []
            for i in range(n):
                @blk.tensor
                def _(tensor, i=i):
                    if i == 0 and first_wait:
                        for nm, v in first_wait:
                            wv(tensor, nm, v)
                    wv(tensor, "sV", psT_busy[i % 2])
                    ins = tensor.transpose(psT[i % 2][:, :], src_fn(i), identt[:, :])
                    tr_vals.append(inc("sP", ins))

                @blk.vector
                def _(vector, i=i):
                    vector.wait_ge(sems["sP"], tr_vals[i])
                    ins = vector.tensor_copy(dst_fn(i), psT[i % 2][:, :])
                    ev_vals.append(inc("sV", ins))
                psT_busy[i % 2] = ev_vals[-1]
            return ev_vals[-1]

        stage_busy = {}

        def load_cast(n, src_fn, dst_fn, stage, sync_first_wait=None):
            ld_vals = []
            cast_base = counts["sV"]
            keys = [stage[0].name, stage[1].name]

            @blk.sync
            def _(sync):
                if sync_first_wait:
                    for nm, v in sync_first_wait:
                        wv(sync, nm, v)
                for i in range(n):
                    if i >= 2:
                        sync.wait_ge(sems["sV"], cast_base + i - 1)
                    else:
                        wv(sync, "sV", stage_busy.get(keys[i % 2], 0))
                    pn = "sA" if i % 2 == 0 else "sB"
                    sync.dma_start(out=stage[i % 2], in_=src_fn(i)).then_inc(sems[pn], 16)
                    counts[pn] += 16
                    ld_vals.append((pn, counts[pn]))

            @blk.vector
            def _(vector):
                for i in range(n):
                    vector.wait_ge(sems[ld_vals[i][0]], ld_vals[i][1])
                    ins = vector.tensor_copy(dst_fn(i), stage[i % 2])
                    inc("sV", ins)
            for i in range(n):
                stage_busy[keys[i % 2]] = cast_base + i + 1
            return counts["sV"]

        gm_last_ind = [0]

        def global_max(idx, src_cols, dst_cols, wait_pre):
            k = len(src_cols)

            @blk.sync
            def _(sync):
                for nm, v in wait_pre:
                    wv(sync, nm, v)
                wv(sync, "sL", gm_last_ind[0])
                for j, sc in enumerate(src_cols):
                    sync.dma_start(out=rowbuf[0:1, 128 * j:128 * (j + 1)],
                                   in_=sc).then_inc(sems["sL"], 16)
                    counts["sL"] += 16
            ROWS = counts["sL"]

            @blk.vector
            def _(vector):
                vector.wait_ge(sems["sL"], ROWS)
                for j in range(k):
                    vector.reduce_max(rowbuf[0:1, 256 + j:257 + j],
                                      rowbuf[0:1, 128 * j:128 * (j + 1)], AX.X)
                vsep(vector)
                for j in range(k):
                    vector.tensor_scalar_mul(rowbuf[0:1, 128 * j:128 * (j + 1)],
                                             onest[0:1, 0:128],
                                             rowbuf[0:1, 256 + j:257 + j])
                inc("sV", vsettle(vector))
            REP = counts["sV"]

            @blk.sync
            def _(sync):
                sync.wait_ge(sems["sV"], REP)
                sync.dma_start(out=cc_in[idx].ap()[:, 0:128 * k],
                               in_=rowbuf[0:1, 0:128 * k]).then_inc(sems["sL"], 16)
                counts["sL"] += 16
            gm_last_ind[0] = counts["sL"]
            IND = counts["sL"]

            @blk.gpsimd
            def _(gpsimd):
                gpsimd.wait_ge(sems["sL"], IND)
                gpsimd.collective_compute(
                    "AllReduce", OP.max, replica_groups=RG,
                    ins=[cc_in[idx].ap()[:, 0:128 * k].opt()],
                    outs=[cc_out[idx].ap()[:, 0:128 * k].opt()],
                ).then_inc(sems["sC"], 1)
                counts["sC"] += 1
                gpsimd.wait_ge(sems["sC"], counts["sC"])
                for j, dc in enumerate(dst_cols):
                    gpsimd.dma_start(
                        out=dc,
                        in_=cc_out[idx].ap()[:, 128 * j:128 * (j + 1)].rearrange(
                            "a b -> b a")).then_inc(sems["sGM"], 16)
                    counts["sGM"] += 16
            return counts["sGM"]

        atn = contextlib.ExitStack()
        Vb = atn.enter_context(nc.sbuf_tensor("Vb", [128, NT, D], BF16, side="right"))
        POFF = []
        off = 0
        for jt in range(NT):
            POFF.append(off)
            off += T - 128 * jt
        PSZ = off
        Pb = atn.enter_context(nc.sbuf_tensor("Pb", [128, PSZ], BF16, side="right"))
        Lst = atn.enter_context(nc.sbuf_tensor("Lst", [128, T], F32, side="right"))
        nmlog = atn.enter_context(nc.sbuf_tensor("nmlog", [128, 2 * NT], F32, side="right")) if debug else None
        lnlog = atn.enter_context(nc.sbuf_tensor("lnlog", [128, 3 * NT], F32, side="right")) if debug else None
        qk_s = contextlib.ExitStack()
        QTb = qk_s.enter_context(nc.sbuf_tensor("QTb", [128, ND, T], BF16, side="right"))
        KTb = qk_s.enter_context(nc.sbuf_tensor("KTb", [128, ND, T], BF16, side="right"))

        dq_s = contextlib.ExitStack()
        wsl = dq_s.enter_context(nc.sbuf_tensor("wsl", [128, O], F32, side="right"))
        wqs = dq_s.enter_context(nc.sbuf_tensor("wqs", [128, O], BF16, side="right"))
        wtr = dq_s.enter_context(nc.sbuf_tensor("wtr", [128, O], BF16, side="right"))
        dstat = dq_s.enter_context(nc.sbuf_tensor("dstat", [128, 192], F32, side="right"))

        @blk.sync
        def _(sync):
            sync.dma_start(out=maskt[:, :], in_=mask_d[:, :]).then_inc(sems["sK"], 16)
            sync.dma_start(out=identt[:, :], in_=ident_d[:, :]).then_inc(sems["sK"], 16)
            sync.dma_start(out=onest[:, :], in_=ones_d[:, :]).then_inc(sems["sK"], 16)
            sync.dma_start(out=biast[:, :], in_=biases_d[:, :]).then_inc(sems["sK"], 16)
            sync.dma_start(out=stat[:, 26:28], in_=eps_d[:, :]).then_inc(sems["sK"], 16)
            counts["sK"] += 80
            sync.dma_start(out=wsl[:].rearrange("p (a b) -> p a b", b=128),
                           in_=wup_d.rearrange("(a p) b -> p a b", p=128)).then_inc(sems["sL"], 16)
            counts["sL"] += 16
        LD_WUP = counts["sL"]

        def emit_dequant(ld_val, pre=None):
            @blk.vector
            def _(vector):
                if pre is not None:
                    wv(vector, pre[0], pre[1])
                vector.wait_ge(sems["sL"], ld_val)
                w3 = wsl[:].rearrange("p (a k) -> p a k", k=32)
                bmax = dstat[:, 0:96]
                s_ = dstat[:, 96:192]
                vector.tensor_reduce(bmax, w3, AX.X, OP.max, apply_absolute_value=True)
                vsep(vector)
                vector.tensor_scalar(s_, bmax, WQMAX_INV, 1e-12, OP.mult, OP.max)
                vsep(vector)
                vrecip(vector, bmax, s_)
                vsep(vector)
                rs3 = bmax.unsqueeze(2).broadcast_to([128, 96, 32])
                s3 = s_.unsqueeze(2).broadcast_to([128, 96, 32])
                vector.tensor_tensor(w3, w3, rs3, OP.mult)
                vector.tensor_scalar(w3, w3, CR, CR, OP.add, OP.subtract)
                o3 = wqs[:].rearrange("p (a k) -> p a k", k=32)
                ins = vector.tensor_tensor(o3, w3, s3, OP.mult)
                inc("sV", ins)
            return counts["sV"]

        wqs24 = wqs[:].rearrange("p (a b) -> p a b", b=128)
        DQ_UP = emit_dequant(LD_WUP)
        TR_UP = transpose_seq(
            NO, lambda i: wqs24[:, i, :],
            lambda i: wtr[:, 128 * i:128 * (i + 1)],
            first_wait=[("sV", DQ_UP), ("sK", 80)])

        @blk.gpsimd
        def _(gpsimd):
            gpsimd.wait_ge(sems["sV"], TR_UP)
            gpsimd.dma_start(out=agw_in.ap()[:, 0:O], in_=wtr[:, :]).then_inc(sems["sG"], 16)
            counts["sG"] += 16
        WUP_D = counts["sG"]

        @blk.sync
        def _(sync):
            sync.wait_ge(sems["sV"], DQ_UP)
            sync.dma_start(out=wsl[:].rearrange("p (a b) -> p a b", b=128),
                           in_=wgate_d.rearrange("(a p) b -> p a b", p=128)).then_inc(sems["sL"], 16)
            counts["sL"] += 16
        LD_WGATE = counts["sL"]

        DQ_GATE = emit_dequant(LD_WGATE, pre=("sG", WUP_D))
        TR_GATE = transpose_seq(
            NO, lambda i: wqs24[:, i, :],
            lambda i: wtr[:, 128 * i:128 * (i + 1)],
            first_wait=[("sV", DQ_GATE), ("sG", WUP_D)])

        @blk.gpsimd
        def _(gpsimd):
            gpsimd.wait_ge(sems["sV"], TR_GATE)
            gpsimd.dma_start(out=agw_in.ap()[:, O:2 * O], in_=wtr[:, :]).then_inc(sems["sG"], 16)
            counts["sG"] += 16
            gpsimd.wait_ge(sems["sG"], counts["sG"])
            gpsimd.collective_compute(
                "AllGather", OP.bypass, replica_groups=RG,
                ins=[agw_in.ap().opt()], outs=[agw_out.ap().opt()],
            ).then_inc(sems["sC"], 1)
            counts["sC"] += 1
        WGATE_D = counts["sG"]
        AGW_DONE = counts["sC"]

        @blk.sync
        def _(sync):
            sync.wait_ge(sems["sV"], DQ_GATE)
            sync.dma_start(out=wsl[:].rearrange("p (a b) -> p a b", b=384),
                           in_=wdown_d.rearrange("(a p) b -> p a b", p=128)).then_inc(sems["sL"], 16)
            counts["sL"] += 16
        LD_WDOWN = counts["sL"]

        DQ_DOWN = emit_dequant(LD_WDOWN, pre=("sG", WGATE_D))
        wqs_d3 = wqs[:].rearrange("p (a b) -> p a b", b=384)
        wtr_d3 = wtr[:].rearrange("p (a b) -> p a b", b=1024)

        def _src_dn(i):
            ht, oc = divmod(i, 3)
            return wqs_d3[:, ht, 128 * oc:128 * (oc + 1)]

        def _dst_dn(i):
            ht, oc = divmod(i, 3)
            return wtr_d3[:, oc, 128 * ht:128 * (ht + 1)]

        TR_DOWN = transpose_seq(NO, _src_dn, _dst_dn, first_wait=[("sV", DQ_DOWN)])

        @blk.gpsimd
        def _(gpsimd):
            gpsimd.wait_ge(sems["sV"], TR_DOWN)
            gpsimd.dma_start(out=agd_in.ap().rearrange("(a p) b -> p a b", p=128),
                             in_=wtr_d3).then_inc(sems["sG"], 16)
            counts["sG"] += 16
            gpsimd.wait_ge(sems["sG"], counts["sG"])
            gpsimd.collective_compute(
                "AllGather", OP.bypass, replica_groups=RG,
                ins=[agd_in.ap().opt()], outs=[agd_out.ap().opt()],
            ).then_inc(sems["sC"], 1)
            counts["sC"] += 1
        AGD_DONE = counts["sC"]

        barrier()
        dq_s.close()

        a1 = contextlib.ExitStack()
        xTb = a1.enter_context(nc.sbuf_tensor("xTb", [128, ND, T], BF16, side="right"))
        wstage = [a1.enter_context(nc.sbuf_tensor(f"wst{i}", [128, D], F32, side="right"))
                  for i in range(2)]
        wbf = a1.enter_context(nc.sbuf_tensor("wbf", [128, ND, D], BF16, side="right"))

        XT_CAST = load_cast(
            2 * ND,
            lambda i: xT_d[128 * (i // 2):128 * (i // 2 + 1), 1024 * (i % 2):1024 * (i % 2 + 1)],
            lambda i: xTb[:, i // 2, 1024 * (i % 2):1024 * (i % 2 + 1)],
            [wstage[0][:, :], wstage[1][:, :]])

        WQ_CAST = load_cast(
            ND, lambda i: wq_d[128 * i:128 * (i + 1), :],
            lambda i: wbf[:, i, :], [wstage[0][:, :], wstage[1][:, :]])

        def qk_group(bias_row, kt, ic):
            def emit(tensor, psum):
                for dt in range(ND):
                    tensor.matmul(psum[:, :], wbf[:, dt, 128 * kt:128 * (kt + 1)],
                                  xTb[:, dt, 512 * ic:512 * (ic + 1)],
                                  start=(dt == 0), stop=False)
                return tensor.matmul(
                    psum[:, :],
                    biast[32 * bias_row:32 * bias_row + 1, 128 * kt:128 * (kt + 1)],
                    onest[32 * bias_row:32 * bias_row + 1, :], start=False, stop=True)
            return emit

        if debug:
            @blk.sync
            def _(sync):
                sync.wait_ge(sems["sV"], XT_CAST)
                sync.dma_start(out=dbg_xt[:, :, :], in_=xTb[:, :, :]).then_inc(sems["sO"], 16)
                counts["sO"] += 16

        qg = [(kt, ic) for kt in range(ND) for ic in range(4)]
        mm_groups([qk_group(0, kt, ic) for kt, ic in qg], ps[0:6], "scalar",
                  lambda eng, g, psum: eng.copy(
                      QTb[:, qg[g][0], 512 * qg[g][1]:512 * (qg[g][1] + 1)], psum[:, :]),
                  pre_wait=[("sV", WQ_CAST), ("sV", XT_CAST)])
        Q_MM = counts["sP"]

        WK_CAST = load_cast(
            ND, lambda i: wk_d[128 * i:128 * (i + 1), :],
            lambda i: wbf[:, i, :], [wstage[0][:, :], wstage[1][:, :]],
            sync_first_wait=[("sP", Q_MM)])

        mm_groups([qk_group(1, kt, ic) for kt, ic in qg], ps[0:6], "scalar",
                  lambda eng, g, psum: eng.copy(
                      KTb[:, qg[g][0], 512 * qg[g][1]:512 * (qg[g][1] + 1)], psum[:, :]),
                  pre_wait=[("sV", WK_CAST)])
        K_MM = counts["sP"]

        WV_CAST = load_cast(
            ND, lambda i: wv_d[128 * i:128 * (i + 1), :],
            lambda i: wbf[:, i, :], [wstage[0][:, :], wstage[1][:, :]],
            sync_first_wait=[("sP", K_MM)])

        def v_group(jt, vc):
            def emit(tensor, psum):
                for dt in range(ND):
                    tensor.matmul(psum[:, :], xTb[:, dt, 128 * jt:128 * (jt + 1)],
                                  wbf[:, dt, 512 * vc:512 * (vc + 1)],
                                  start=(dt == 0), stop=False)
                return tensor.matmul(
                    psum[:, :], onest[64:65, 0:128],
                    biast[64:65, 512 * vc:512 * (vc + 1)],
                    start=False, stop=True)
            return emit

        vg = [(jt, vc) for jt in range(NT) for vc in range(2)]
        mm_groups([v_group(jt, vc) for jt, vc in vg], ps[0:6], "scalar",
                  lambda eng, g, psum: eng.copy(
                      Vb[:, vg[g][0], 512 * vg[g][1]:512 * (vg[g][1] + 1)], psum[:, :]),
                  pre_wait=[("sV", WV_CAST)])

        barrier()
        a1.close()

        jt_exp = {}
        for jt in range(NT):
            Li = T - 128 * jt
            nch = (Li + 511) // 512

            def l_group(jt, c):
                def emit(tensor, psum):
                    i0 = 128 * jt + 512 * c
                    n = min(512, T - i0)
                    last = None
                    for kt in range(ND):
                        last = tensor.matmul(
                            psum[:, 0:n], KTb[:, kt, 128 * jt:128 * (jt + 1)],
                            QTb[:, kt, i0:i0 + n],
                            start=(kt == 0), stop=(kt == ND - 1))
                    return last
                return emit

            def mk_evac(jt):
                def evac(eng, g, psum):
                    if g == 0 and jt >= 1:
                        eng.wait_ge(sems["sS"], jt_exp[jt - 1])
                    i0r = 512 * g
                    n = min(512, (T - 128 * jt) - i0r)
                    return eng.tensor_copy(Lst[:, i0r:i0r + n], psum[:, 0:n])
                return evac

            mm_groups([l_group(jt, c) for c in range(nch)], ps[0:6], "vector",
                      mk_evac(jt))
            LJT = counts["sV"]

            @blk.vector
            def _(vector, jt=jt, Li=Li, LJT=LJT):
                vector.tensor_tensor(Lst[:, 0:128], Lst[:, 0:128], maskt[:, :], OP.add)
                vector.tensor_reduce(col(S_NM), Lst[:, 0:Li], AX.X, OP.max,
                                     negate=True)
                vector.tensor_copy(stat[:, S_JNK:S_JNK + 1], stat[:, 26:27])
                ins = vector.tensor_copy(stat[:, S_JNK:S_JNK + 1], stat[:, 26:27])
                inc("sV", ins)
            NM = counts["sV"]

            @blk.scalar
            def _(scalar, jt=jt, Li=Li, NM=NM):
                scalar.wait_ge(sems["sV"], NM)
                scalar.activation(Pb[:, POFF[jt]:POFF[jt] + Li], Lst[:, 0:Li],
                                  ACT.Exp, bias=col(S_NM), scale=1.0,
                                  accum_out=col(S_SSUM))
                scalar.activation(stat[:, S_JNK:S_JNK + 1], stat[:, 26:27], ACT.Exp)
                ins = scalar.activation(stat[:, S_JNK:S_JNK + 1], stat[:, 26:27], ACT.Exp)
                jt_exp[jt] = inc("sS", ins)

            @blk.vector
            def _(vector, jt=jt):
                vector.wait_ge(sems["sS"], jt_exp[jt])
                if debug:
                    vector.tensor_copy(nmlog[:, jt:jt + 1], col(S_NM))
                    vector.tensor_copy(nmlog[:, NT + jt:NT + jt + 1], col(S_SSUM))
                ins = vector.tensor_copy(rcol[:, jt:jt + 1], col(S_SSUM))
                inc("sV", ins)

        if debug:
            LAST_EXP = jt_exp[NT - 1]
            @blk.sync
            def _(sync):
                sync.wait_ge(sems["sS"], LAST_EXP)
                sync.dma_start(out=dbg_l[:, :], in_=Lst[:, :]).then_inc(sems["sO"], 16)
                sync.dma_start(out=dbg_nm[:, :], in_=nmlog[:, :]).then_inc(sems["sO"], 16)
                counts["sO"] += 32

        if debug:
            A2_DONE_V = counts["sV"]
            @blk.sync
            def _(sync):
                sync.wait_ge(sems["sV"], A2_DONE_V)
                sync.dma_start(out=dbg_q[:, :, :], in_=QTb[:, :, :]).then_inc(sems["sO"], 16)
                sync.dma_start(out=dbg_k[:, :, :], in_=KTb[:, :, :]).then_inc(sems["sO"], 16)
                sync.dma_start(out=dbg_v[:, :, :], in_=Vb[:, :, :]).then_inc(sems["sO"], 16)
                sync.dma_start(out=dbg_p[:, :], in_=Pb[:, :]).then_inc(sems["sO"], 16)
                sync.dma_start(out=dbg_rcol[:, :], in_=rcol[:, :]).then_inc(sems["sO"], 16)
                counts["sO"] += 80

        @blk.vector
        def _(vector):
            t32 = Lst[:, 1024:1024 + NT]
            vsep(vector)
            vector.tensor_scalar_mul(t32, rcol[:, :], 32.0)
            vsep8(vector)
            vrecip(vector, rcol[:, :], t32)
            vsep8(vector)
            ins = None
            for jt in range(NT):
                ins = vector.tensor_scalar_mul(Vb[:, jt, :], Vb[:, jt, :],
                                               rcol[:, jt:jt + 1])
            inc("sV", ins)

        barrier()
        if probe:
            @blk.sync
            def _(sync):
                sync.dma_start(out=dbg_pp[:, :], in_=Pb[:, :]).then_inc(sems["sO"], 16)
                sync.dma_start(out=dbg_pv[:, :, :], in_=Vb[:, :, :]).then_inc(sems["sO"], 16)
                sync.dma_start(out=dbg_pr[:, :], in_=rcol[:, :]).then_inc(sems["sO"], 16)
                counts["sO"] += 48
        qk_s.close()

        x1 = ctx.enter_context(nc.sbuf_tensor("x1", [128, NT, D], F32, side="left"))

        x_ld = []
        @blk.sync
        def _(sync):
            for it in range(NT):
                sync.dma_start(out=x1[:, it, :],
                               in_=x_d[128 * it:128 * (it + 1), :]).then_inc(sems["sL"], 16)
                counts["sL"] += 16
                x_ld.append(counts["sL"])

        pvg = [(it, vh) for it in range(NT) for vh in range(2)]

        def pv_group(it, vh):
            def emit(tensor, psum):
                last = None
                for jt in range(it + 1):
                    lhs = Pb[:, POFF[jt] + 128 * (it - jt):POFF[jt] + 128 * (it - jt) + 128]
                    last = tensor.matmul(psum[:, :], lhs,
                                         Vb[:, jt, 512 * vh:512 * (vh + 1)],
                                         start=(jt == 0), stop=(jt == it))
                return last
            return emit

        def pv_evac(eng, g, psum):
            it, vh = pvg[g]
            eng.wait_ge(sems["sL"], x_ld[-1])
            sl = x1[:, it, 512 * vh:512 * (vh + 1)]
            return eng.scalar_tensor_tensor(sl, sl, 2.0, psum[:, :], OP.mult, OP.add)

        pv_recs = mm_groups([pv_group(it, vh) for it, vh in pvg], ps[0:6], "vector",
                            pv_evac)

        PROBE_Y = [0]
        if probe:
            @blk.sync
            def _(sync):
                sync.wait_ge(sems["sV"], pv_recs[-1][1])
                for it in range(NT):
                    sync.dma_start(out=dbg_py[128 * it:128 * (it + 1), :],
                                   in_=x1[:, it, :]).then_inc(sems["sO"], 16)
                    counts["sO"] += 16
            PROBE_Y[0] = counts["sO"]

        MS16 = Lst[:, 1056:1056 + NT]
        SQ16 = Lst[:, 1056 + NT:1056 + 2 * NT]
        M16 = Lst[:, 1056 + 2 * NT:1056 + 3 * NT]
        V16 = Lst[:, 1056 + 3 * NT:1056 + 4 * NT]
        SD16 = Lst[:, 1056 + 4 * NT:1056 + 5 * NT]
        I16 = Lst[:, 1056 + 5 * NT:1056 + 6 * NT]

        for it in range(NT):
            y_done = pv_recs[2 * it + 1][1]

            @blk.vector
            def _(vector, it=it, y_done=y_done):
                vector.wait_ge(sems["sV"], y_done)
                vector.reduce_sum(col(S_MS), x1[:, it, :], AX.X)
                vector.tensor_mul(Lst[:, 0:D], x1[:, it, :], x1[:, it, :])
                vector.tensor_copy(MS16[:, it:it + 1], col(S_MS))
                ins = vector.reduce_sum(SQ16[:, it:it + 1], Lst[:, 0:D], AX.X)
                inc("sV", ins)

        @blk.vector
        def _(vector):
            vsep8(vector)
            vector.tensor_scalar_mul(M16, MS16, 1.0 / D)
            vector.tensor_scalar_mul(V16, SQ16, 1.0 / D)
            vsep(vector)
            vector.tensor_mul(SD16, M16, M16)
            vsep(vector)
            vector.tensor_sub(V16, V16, SD16)
            inc("sV", vsettle(vector))
        VARB = counts["sV"]

        @blk.scalar
        def _(scalar):
            scalar.wait_ge(sems["sV"], VARB)
            scalar.activation(SD16, V16, ACT.Sqrt, bias=stat[:, 26:27], scale=1.0)
            inc("sS", ssettle(scalar))
        STDB = counts["sS"]

        @blk.vector
        def _(vector):
            vector.wait_ge(sems["sS"], STDB)
            vrecip(vector, I16, SD16)
            vsep8(vector)
            for it in range(NT):
                vector.tensor_scalar(x1[:, it, :], x1[:, it, :],
                                     M16[:, it:it + 1], I16[:, it:it + 1],
                                     OP.subtract, OP.mult)
                vector.tensor_reduce(col(S_T0), x1[:, it, :], AX.X, OP.max,
                                     apply_absolute_value=True)
                vsep(vector)
                if it == 0:
                    vector.tensor_copy(col(S_GMX), col(S_T0))
                else:
                    vector.tensor_max(col(S_GMX), col(S_GMX), col(S_T0))
            inc("sV", vsettle(vector))
        ln1_last = counts["sV"]

        if debug:
            @blk.sync
            def _(sync):
                sync.wait_ge(sems["sV"], ln1_last)
                for it in range(NT):
                    sync.dma_start(out=dbg_x1[128 * it:128 * (it + 1), :],
                                   in_=x1[:, it, :]).then_inc(sems["sO"], 16)
                    counts["sO"] += 16

        barrier()
        if probe:
            @blk.sync
            def _(sync):
                for it in range(NT):
                    sync.dma_start(out=dbg_px[128 * it:128 * (it + 1), :],
                                   in_=x1[:, it, :]).then_inc(sems["sO"], 16)
                    counts["sO"] += 16
                sync.dma_start(out=dbg_arr[:, :], in_=Lst[:, 1056:1152]).then_inc(sems["sO"], 16)
                counts["sO"] += 16
        atn.close()

        ffn = contextlib.ExitStack()
        gateb = ffn.enter_context(nc.sbuf_tensor("gateb", [128, NO, T], BF16, side="right"))
        ffn_xq = contextlib.ExitStack()
        xqT = ffn_xq.enter_context(nc.sbuf_tensor("xqT", [128, ND, T], BF16, side="right"))
        a4_s = contextlib.ExitStack()
        fscr = a4_s.enter_context(nc.sbuf_tensor("fscr", [128, D], F32, side="right"))
        xqst = a4_s.enter_context(nc.sbuf_tensor("xqst", [128, D], BF16, side="right"))

        GM1 = global_max(0, [col(S_GMX)], [col(S_GMX)], [("sV", ln1_last)])

        @blk.vector
        def _(vector):
            vector.wait_ge(sems["sGM"], GM1)
            vector.tensor_scalar(col(S_SX), col(S_GMX), INV_QMAX, 1e-12, OP.mult, OP.max)
            vsep(vector)
            vrecip(vector, col(S_RSX), col(S_SX))
            vector.tensor_copy(stat[:, S_JNK:S_JNK + 1], stat[:, 26:27])
            ins = vector.tensor_copy(stat[:, S_JNK:S_JNK + 1], stat[:, 26:27])
            inc("sV", ins)
        SXD = counts["sV"]

        xq_v = {}
        xq_ev = {-1: 0}
        for it in range(NT):
            @blk.scalar
            def _(scalar, it=it):
                if it == 0:
                    scalar.wait_ge(sems["sV"], SXD)
                else:
                    scalar.wait_ge(sems["sV"], xq_v[it - 1])
                ins = scalar.activation(fscr[:, :], x1[:, it, :], ACT.Identity,
                                        bias=stat[:, 27:28], scale=col(S_RSX))
                inc("sS", ins)
            TQ = counts["sS"]

            @blk.vector
            def _(vector, it=it, TQ=TQ):
                vector.wait_ge(sems["sS"], TQ)
                if it >= 1:
                    vector.wait_ge(sems["sV"], xq_ev[it - 1])
                ins = vector.tensor_scalar(xqst[:, :], fscr[:, :],
                                           CR, col(S_SX), OP.subtract, OP.mult)
                xq_v[it] = inc("sV", ins)

            xq_ev[it] = transpose_seq(
                ND,
                lambda kt: xqst[:, 128 * kt:128 * (kt + 1)],
                lambda kt, it=it: xqT[:, kt, 128 * it:128 * (it + 1)],
                first_wait=[("sV", xq_v[it])])

        barrier()
        a4_s.close()

        b_s = contextlib.ExitStack()
        wtile = [b_s.enter_context(nc.sbuf_tensor(f"wtile{i}", [128, ND, 128], BF16, side="right"))
                 for i in range(2)]
        upst = b_s.enter_context(nc.sbuf_tensor("upst", [128, T], BF16, side="right"))

        border = [(w_i, ot) for ot in range(NO) for w_i in range(2)]
        SP_B_BASE = counts["sP"]
        B_PE = [SP_B_BASE + 4 * (j + 1) for j in range(2 * NO)]
        wt_ld = []

        @blk.sync
        def _(sync):
            sync.wait_ge(sems["sC"], AGW_DONE)
            for j, (w_i, ot) in enumerate(border):
                if j >= 2:
                    sync.wait_ge(sems["sP"], B_PE[j - 2])
                ci = w_i * NO + ot
                pn = "sA" if j % 2 == 0 else "sB"
                sync.dma_start(
                    out=wtile[j % 2][:, :, :],
                    in_=agw_out.ap()[:, 128 * ci:128 * (ci + 1)].rearrange(
                        "(ht p) o -> p ht o", p=128)).then_inc(sems[pn], 16)
                counts[pn] += 16
                wt_ld.append((pn, counts[pn]))

        up_dma = {}
        up_max = {}
        gmax_seed = [False, False]
        for j, (w_i, ot) in enumerate(border):
            def b_group(j, ic):
                def emit(tensor, psum):
                    last = None
                    for ht in range(ND):
                        last = tensor.matmul(psum[:, :], wtile[j % 2][:, ht, :],
                                             xqT[:, ht, 512 * ic:512 * (ic + 1)],
                                             start=(ht == 0), stop=(ht == ND - 1))
                    return last
                return emit

            groups = [b_group(j, ic) for ic in range(4)]

            if w_i == 0:
                def mk_ev_up(ot):
                    def evac(eng, g, psum):
                        if g == 0 and ot >= 1:
                            eng.wait_ge(sems["sG"], up_dma[ot - 1])
                            eng.wait_ge(sems["sV"], up_max[ot - 1])
                        return eng.copy(upst[:, 512 * g:512 * (g + 1)], psum[:, :])
                    return evac
                recs = mm_groups(groups, ps[0:6], "scalar", mk_ev_up(ot),
                                 pre_wait=[wt_ld[j]])
                EV = recs[-1][1]

                @blk.gpsimd
                def _(gpsimd, ot=ot, EV=EV):
                    gpsimd.wait_ge(sems["sS"], EV)
                    gpsimd.dma_start(out=up_hbm.ap()[ot, :, :],
                                     in_=upst[:, :]).then_inc(sems["sG"], 16)
                    counts["sG"] += 16
                    up_dma[ot] = counts["sG"]

                @blk.vector
                def _(vector, ot=ot, EV=EV):
                    vector.wait_ge(sems["sS"], EV)
                    vector.tensor_reduce(col(S_T0), upst[:, :], AX.X, OP.max,
                                         apply_absolute_value=True)
                    vsep(vector)
                    if not gmax_seed[0]:
                        vector.tensor_copy(col(S_GU), col(S_T0))
                        gmax_seed[0] = True
                    else:
                        vector.tensor_max(col(S_GU), col(S_GU), col(S_T0))
                    up_max[ot] = inc("sV", vsettle(vector))
            else:
                def mk_ev_g(ot):
                    def evac(eng, g, psum):
                        return eng.copy(gateb[:, ot, 512 * g:512 * (g + 1)], psum[:, :])
                    return evac
                recs = mm_groups(groups, ps[0:6], "scalar", mk_ev_g(ot),
                                 pre_wait=[wt_ld[j]])
                EV = recs[-1][1]

                @blk.vector
                def _(vector, ot=ot, EV=EV):
                    vector.wait_ge(sems["sS"], EV)
                    vector.tensor_reduce(col(S_T0), gateb[:, ot, :], AX.X, OP.max,
                                         apply_absolute_value=True)
                    vsep(vector)
                    if not gmax_seed[1]:
                        vector.tensor_copy(col(S_GG), col(S_T0))
                        gmax_seed[1] = True
                    else:
                        vector.tensor_max(col(S_GG), col(S_GG), col(S_T0))
                    inc("sV", vsettle(vector))
        B_V_LAST = counts["sV"]

        barrier()
        b_s.close()
        ffn_xq.close()

        GM2 = global_max(1, [col(S_GU), col(S_GG)], [col(S_GU), col(S_GG)],
                         [("sV", B_V_LAST)])

        @blk.vector
        def _(vector):
            vector.wait_ge(sems["sGM"], GM2)
            vector.tensor_scalar(col(S_SU), col(S_GU), INV_QMAX, 1e-12, OP.mult, OP.max)
            vector.tensor_scalar(col(S_SG), col(S_GG), INV_QMAX, 1e-12, OP.mult, OP.max)
            vsep(vector)
            vrecip(vector, col(S_RSU), col(S_SU))
            vsep(vector)
            vrecip(vector, col(S_RSG), col(S_SG))
            vector.tensor_copy(stat[:, S_JNK:S_JNK + 1], stat[:, 26:27])
            ins = vector.tensor_copy(stat[:, S_JNK:S_JNK + 1], stat[:, 26:27])
            inc("sV", ins)
        SC2 = counts["sV"]

        c_s = contextlib.ExitStack()
        scrA = [c_s.enter_context(nc.sbuf_tensor(f"scrA{i}", [128, T], F32, side="right")) for i in range(2)]
        if debug:
            _scrB0 = c_s.enter_context(nc.sbuf_tensor("scrB0", [128, T], F32, side="right"))
            scrB = [_scrB0, _scrB0]
        else:
            scrB = [c_s.enter_context(nc.sbuf_tensor(f"scrB{i}", [128, T], F32, side="right")) for i in range(2)]
        upld = [c_s.enter_context(nc.sbuf_tensor(f"upld{i}", [128, T], BF16, side="right")) for i in range(2)]

        c1a_v = {}
        for ot in range(NO):
            A = scrA[ot % 2]

            @blk.scalar
            def _(scalar, ot=ot, A=A):
                if ot == 0:
                    scalar.wait_ge(sems["sV"], SC2)
                if ot >= 2:
                    scalar.wait_ge(sems["sV"], c1a_v[ot - 2])
                ins = scalar.activation(A[:, :], gateb[:, ot, :], ACT.Identity,
                                        bias=stat[:, 27:28], scale=col(S_RSG))
                inc("sS", ins)
            T1 = counts["sS"]

            @blk.vector
            def _(vector, ot=ot, A=A, T1=T1):
                vector.wait_ge(sems["sS"], T1)
                ins = vector.tensor_scalar(gateb[:, ot, :], A[:, :], CR, col(S_SG),
                                           OP.subtract, OP.mult)
                c1a_v[ot] = inc("sV", ins)
        C1A_LAST = counts["sV"]

        c1_v = {}
        for ot in range(NO):
            A = scrA[ot % 2]
            Bs = scrB[ot % 2]

            @blk.scalar
            def _(scalar, ot=ot, A=A):
                if ot == 0:
                    scalar.wait_ge(sems["sV"], C1A_LAST)
                if ot >= 2:
                    scalar.wait_ge(sems["sV"], c1_v[ot - 2])
                ins = scalar.activation(A[:, :], gateb[:, ot, :], ACT.Sigmoid)
                inc("sS", ins)
            SGD = counts["sS"]

            @blk.vector
            def _(vector, ot=ot, A=A, Bs=Bs, SGD=SGD):
                vector.wait_ge(sems["sS"], SGD)
                vector.tensor_scalar(Bs[:, :], A[:, :], 65536.0, CR, OP.mult, OP.add)
                vector.tensor_scalar(Bs[:, :], Bs[:, :], CR, 65535.0, OP.subtract, OP.min)
                vector.scalar_tensor_tensor(gateb[:, ot, :], gateb[:, ot, :],
                                            1.0 / 65536.0, Bs[:, :], OP.mult, OP.mult)
                vector.tensor_reduce(col(S_T0), gateb[:, ot, :], AX.X, OP.max,
                                     apply_absolute_value=True)
                vsep(vector)
                if ot == 0:
                    vector.tensor_copy(col(S_G2M), col(S_T0))
                else:
                    vector.tensor_max(col(S_G2M), col(S_G2M), col(S_T0))
                c1_v[ot] = inc("sV", vsettle(vector))
        C1_LAST = counts["sV"]

        GM3 = global_max(2, [col(S_G2M)], [col(S_G2M)], [("sV", C1_LAST)])

        @blk.vector
        def _(vector):
            vector.wait_ge(sems["sGM"], GM3)
            vector.tensor_scalar(col(S_SG2), col(S_G2M), INV_QMAX, 1e-12, OP.mult, OP.max)
            vsep(vector)
            vrecip(vector, col(S_RSG2), col(S_SG2))
            vector.tensor_copy(stat[:, S_JNK:S_JNK + 1], stat[:, 26:27])
            ins = vector.tensor_copy(stat[:, S_JNK:S_JNK + 1], stat[:, 26:27])
            inc("sV", ins)
        SC3 = counts["sV"]

        c2_v = {}
        up_rld = {}
        for ot in range(NO):
            A = scrA[ot % 2]
            Bs = scrB[ot % 2]

            @blk.gpsimd
            def _(gpsimd, ot=ot):
                if ot >= 2:
                    gpsimd.wait_ge(sems["sV"], c2_v[ot - 2])
                pn = "sA" if ot % 2 == 0 else "sB"
                gpsimd.dma_start(out=upld[ot % 2][:, :],
                                 in_=up_hbm.ap()[ot, :, :]).then_inc(sems[pn], 16)
                counts[pn] += 16
                up_rld[ot] = (pn, counts[pn])

            @blk.scalar
            def _(scalar, ot=ot, A=A, Bs=Bs):
                if ot == 0:
                    scalar.wait_ge(sems["sV"], SC3)
                if ot >= 2:
                    scalar.wait_ge(sems["sV"], c2_v[ot - 2])
                scalar.activation(A[:, :], gateb[:, ot, :], ACT.Identity,
                                  bias=stat[:, 27:28], scale=col(S_RSG2))
                scalar.wait_ge(sems[up_rld[ot][0]], up_rld[ot][1])
                ins = scalar.activation(Bs[:, :], upld[ot % 2][:, :], ACT.Identity,
                                        bias=stat[:, 27:28], scale=col(S_RSU))
                inc("sS", ins)
            TS = counts["sS"]

            @blk.vector
            def _(vector, ot=ot, A=A, Bs=Bs, TS=TS):
                vector.wait_ge(sems["sS"], TS)
                vector.tensor_scalar(gateb[:, ot, :], A[:, :], CR, col(S_SG2),
                                     OP.subtract, OP.mult)
                vector.tensor_scalar(upld[ot % 2][:, :], Bs[:, :], CR, col(S_SU),
                                     OP.subtract, OP.mult)
                vector.tensor_mul(gateb[:, ot, :], gateb[:, ot, :], upld[ot % 2][:, :])
                vector.tensor_reduce(col(S_T0), gateb[:, ot, :], AX.X, OP.max,
                                     apply_absolute_value=True)
                vsep(vector)
                if ot == 0:
                    vector.tensor_copy(col(S_OM), col(S_T0))
                else:
                    vector.tensor_max(col(S_OM), col(S_OM), col(S_T0))
                c2_v[ot] = inc("sV", vsettle(vector))
        C2_LAST = counts["sV"]

        GM4 = global_max(3, [col(S_OM)], [col(S_OM)], [("sV", C2_LAST)])

        @blk.vector
        def _(vector):
            vector.wait_ge(sems["sGM"], GM4)
            vector.tensor_scalar(col(S_SO), col(S_OM), INV_QMAX, 1e-12, OP.mult, OP.max)
            vsep(vector)
            vrecip(vector, col(S_RSO), col(S_SO))
            vector.tensor_copy(stat[:, S_JNK:S_JNK + 1], stat[:, 26:27])
            ins = vector.tensor_copy(stat[:, S_JNK:S_JNK + 1], stat[:, 26:27])
            inc("sV", ins)
        SC4 = counts["sV"]

        o_v = {}
        for ot in range(NO):
            A = scrA[ot % 2]

            @blk.scalar
            def _(scalar, ot=ot, A=A):
                if ot == 0:
                    scalar.wait_ge(sems["sV"], SC4)
                if ot >= 2:
                    scalar.wait_ge(sems["sV"], o_v[ot - 2])
                ins = scalar.activation(A[:, :], gateb[:, ot, :], ACT.Identity,
                                        bias=stat[:, 27:28], scale=col(S_RSO))
                inc("sS", ins)
            TO = counts["sS"]

            @blk.vector
            def _(vector, ot=ot, A=A, TO=TO):
                vector.wait_ge(sems["sS"], TO)
                ins = vector.tensor_scalar(gateb[:, ot, :], A[:, :], CR, col(S_SO),
                                           OP.subtract, OP.mult)
                o_v[ot] = inc("sV", ins)

        if debug:
            O_DBG = counts["sV"]
            @blk.sync
            def _(sync):
                sync.wait_ge(sems["sV"], O_DBG)
                for ot in range(NO):
                    sync.dma_start(out=dbg_gate[ot, :, :],
                                   in_=gateb[:, ot, :]).then_inc(sems["sO"], 16)
                    counts["sO"] += 16

        barrier()
        c_s.close()

        d_s = contextlib.ExitStack()
        wdh = d_s.enter_context(nc.sbuf_tensor("wdh", [128, NO, 512], BF16, side="right"))
        ln2log = d_s.enter_context(nc.sbuf_tensor("ln2log", [128, 3 * NT], F32, side="right")) if debug else None
        sqscr2 = d_s.enter_context(nc.sbuf_tensor("sqscr2", [128, D + 128], F32, side="right"))

        dn_recs = {}
        DN_PE_H0 = 0
        for h in range(2):
            @blk.sync
            def _(sync, h=h):
                if h == 0:
                    sync.wait_ge(sems["sC"], AGD_DONE)
                else:
                    sync.wait_ge(sems["sP"], DN_PE_H0)
                sync.dma_start(
                    out=wdh[:, :, :],
                    in_=agd_out.ap()[:, 512 * h:512 * (h + 1)].rearrange(
                        "(ot p) o -> p ot o", p=128)).then_inc(sems["sL"], 16)
                counts["sL"] += 16
            WDH = counts["sL"]

            def dn_group(it):
                def emit(tensor, psum):
                    last = None
                    for ot in range(NO):
                        last = tensor.matmul(psum[:, :],
                                             gateb[:, ot, 128 * it:128 * (it + 1)],
                                             wdh[:, ot, :],
                                             start=(ot == 0), stop=(ot == NO - 1))
                    return last
                return emit

            def mk_dn_evac(h):
                def evac(eng, g, psum):
                    sl = x1[:, g, 512 * h:512 * (h + 1)]
                    return eng.tensor_add(sl, sl, psum[:, :])
                return evac

            recs = mm_groups([dn_group(it) for it in range(NT)], ps[0:6], "vector",
                             mk_dn_evac(h), pre_wait=[("sL", WDH)])
            for g in range(NT):
                dn_recs[(h, g)] = recs[g][1]
            if h == 0:
                DN_PE_H0 = counts["sP"]

        if debug:
            Y20 = dn_recs[(1, 0)]
            @blk.sync
            def _(sync):
                sync.wait_ge(sems["sV"], Y20)
                sync.dma_start(out=dbg_y2[:, :], in_=x1[:, 0, :]).then_inc(sems["sO"], 16)
                sync.dma_start(out=dbg_wdh[:, :, :], in_=wdh[:, :, :]).then_inc(sems["sO"], 16)
                counts["sO"] += 32

        MS2 = sqscr2[:, 1024:1024 + NT]
        SQ2c = sqscr2[:, 1024 + NT:1024 + 2 * NT]
        M2c = sqscr2[:, 1024 + 2 * NT:1024 + 3 * NT]
        V2c = sqscr2[:, 1024 + 3 * NT:1024 + 4 * NT]
        SD2c = sqscr2[:, 1024 + 4 * NT:1024 + 5 * NT]
        I2c = sqscr2[:, 1024 + 5 * NT:1024 + 6 * NT]

        for it in range(NT):
            y2 = dn_recs[(1, it)]

            @blk.vector
            def _(vector, it=it, y2=y2):
                vector.wait_ge(sems["sV"], y2)
                vector.reduce_sum(col(S_MS), x1[:, it, :], AX.X)
                vector.tensor_mul(sqscr2[:, 0:D], x1[:, it, :], x1[:, it, :])
                vector.tensor_copy(MS2[:, it:it + 1], col(S_MS))
                ins = vector.reduce_sum(SQ2c[:, it:it + 1], sqscr2[:, 0:D], AX.X)
                inc("sV", ins)

        @blk.vector
        def _(vector):
            vsep8(vector)
            vector.tensor_scalar_mul(M2c, MS2, 1.0 / D)
            vector.tensor_scalar_mul(V2c, SQ2c, 1.0 / D)
            vsep(vector)
            vector.tensor_mul(SD2c, M2c, M2c)
            vsep(vector)
            vector.tensor_sub(V2c, V2c, SD2c)
            inc("sV", vsettle(vector))
        VARB2 = counts["sV"]

        @blk.scalar
        def _(scalar):
            scalar.wait_ge(sems["sV"], VARB2)
            scalar.activation(SD2c, V2c, ACT.Sqrt, bias=stat[:, 26:27], scale=1.0)
            inc("sS", ssettle(scalar))
        STDB2 = counts["sS"]

        @blk.vector
        def _(vector):
            vector.wait_ge(sems["sS"], STDB2)
            vrecip(vector, I2c, SD2c)
            vsep8(vector)
            ins = None
            for it in range(NT):
                ins = vector.tensor_scalar(x1[:, it, :], x1[:, it, :],
                                           M2c[:, it:it + 1], I2c[:, it:it + 1],
                                           OP.subtract, OP.mult)
            inc("sV", ins)
        OUT_RDY = counts["sV"]

        @blk.sync
        def _(sync):
            sync.wait_ge(sems["sV"], OUT_RDY)
            for it in range(NT):
                sync.dma_start(out=y_d[128 * it:128 * (it + 1), :],
                               in_=x1[:, it, :]).then_inc(sems["sO"], 16)
                counts["sO"] += 16

        if debug:
            @blk.sync
            def _(sync):
                sync.dma_start(out=dbg_stat[:, 0:28], in_=stat[:, :]).then_inc(sems["sO"], 16)
                counts["sO"] += 16

        FIN = counts["sO"]

        @blk.sync
        def _(sync):
            sync.wait_ge(sems["sO"], FIN)

        d_s.close()
        ffn.close()
    return nc


def make_in_maps(x, wq, wk, wv, bq, bk, bv, wg, wu, wd):
    import ml_dtypes
    biases = np.zeros((65, D), dtype=np.float32)
    biases[0] = bq
    biases[32] = bk
    biases[64] = bv
    mask = np.where(np.arange(128)[None, :] >= np.arange(128)[:, None],
                    np.float32(0.0), np.float32(MASK_NEG)).astype(ml_dtypes.bfloat16)
    ident = np.eye(128, dtype=ml_dtypes.bfloat16)
    ones512 = np.zeros((65, 512), dtype=np.float32)
    ones512[0] = 1.0
    ones512[32] = 1.0
    ones512[64] = 1.0
    epscol = np.zeros((128, 2), dtype=np.float32)
    epscol[:, 0] = 1e-5
    epscol[:, 1] = CR
    in_maps = []
    for c in range(NC):
        in_maps.append({
            "x": np.ascontiguousarray(x[c]),
            "xT": np.ascontiguousarray(x[c].T),
            "wq": wq, "wk": wk, "wv": wv,
            "biases": biases,
            "wup_s": np.ascontiguousarray(wu[:, 128 * c:128 * (c + 1)]),
            "wgate_s": np.ascontiguousarray(wg[:, 128 * c:128 * (c + 1)]),
            "wdown_s": np.ascontiguousarray(wd[:, 384 * c:384 * (c + 1)]),
            "mask": mask, "ident": ident, "ones512": ones512, "epscol": epscol,
        })
    return in_maps


def kernel(L0_attn_minibatch, L0_attn_Wq, L0_attn_bq, L0_attn_Wk, L0_attn_bk,
           L0_attn_Wv, L0_attn_bv, L0_ffn_w_gate, L0_ffn_w_up, L0_ffn_w_down):
    x = np.asarray(L0_attn_minibatch, np.float32)
    wq = np.ascontiguousarray(np.asarray(L0_attn_Wq, np.float32))
    wk = np.ascontiguousarray(np.asarray(L0_attn_Wk, np.float32))
    wv = np.ascontiguousarray(np.asarray(L0_attn_Wv, np.float32))
    bq = np.asarray(L0_attn_bq, np.float32)
    bk = np.asarray(L0_attn_bk, np.float32)
    bv = np.asarray(L0_attn_bv, np.float32)
    wg = np.asarray(L0_ffn_w_gate, np.float32)
    wu = np.asarray(L0_ffn_w_up, np.float32)
    wd = np.asarray(L0_ffn_w_down, np.float32)

    nc = build()
    in_maps = make_in_maps(x, wq, wk, wv, bq, bk, bv, wg, wu, wd)
    res = run_bass_kernel_spmd(nc, in_maps, core_ids=list(range(NC)))
    return np.stack([np.asarray(res.results[c]["y"], np.float32) for c in range(NC)], 0)
